# version 14
# baseline (speedup 1.0000x reference)
import numpy as np
import ml_dtypes

import concourse.bass as bass
import concourse.mybir as mybir
from concourse.bass_utils import run_bass_kernel_spmd

F32 = mybir.dt.float32
BF16 = mybir.dt.bfloat16
U8 = mybir.dt.uint8
AF = mybir.ActivationFunctionType
ALU = mybir.AluOpType
AX = mybir.AxisListType

T = 2048
D = 4096
MT = 1024
NTT = 16
NMT = 8
DC = 32
H = 4
KV = 512
NG = 8
GT = 256
ISQ_HD = float(1.0 / np.sqrt(128.0))
ISQ_DK = float(1.0 / np.sqrt(32.0))
INV_D = float(1.0 / 4096.0)
EPS = 1e-5
EXPB = -30.0
KB = 1024

CHAIN_SEM = {"vector": "s_ve", "scalar": "s_se"}


class Sched:
    ENGINES = ("sync", "gpsimd", "tensor", "vector", "scalar")

    def __init__(self):
        self.steps = {k: [] for k in self.ENGINES}
        self.tot = {}
        self.lab = {}

    def op(self, eng, fn, label=None, inc=None, n=1, dma=False):
        if eng in CHAIN_SEM and not dma:
            assert inc is None
            sem = CHAIN_SEM[eng]
            t = self.tot.get(sem, 0) + 1
            self.tot[sem] = t
            if label is not None:
                assert (sem, label) not in self.lab
                self.lab[(sem, label)] = t
            self.steps[eng].append(("cop", fn, sem, t))
        else:
            if inc is not None:
                sem, lbl = inc
                t = self.tot.get(sem, 0) + n
                self.tot[sem] = t
                assert (sem, lbl) not in self.lab, (sem, lbl)
                self.lab[(sem, lbl)] = t
            self.steps[eng].append(("op", fn, inc, n))

    def wait(self, eng, sem, label=None, val=None):
        assert (label is None) != (val is None)
        self.steps[eng].append(("wait", sem, label, val))

    def emit(self, eng_name, eng, sems):
        cnt = 0
        for step in self.steps[eng_name]:
            if step[0] == "wait":
                _, sem, label, val = step
                v = val if val is not None else self.lab[(sem, label)]
                eng.wait_ge(sems[sem], v)
            elif step[0] == "cop":
                _, fn, sem, t = step
                if cnt > 0:
                    eng.wait_ge(sems[sem], cnt)
                instr = fn(eng)
                assert instr is not None
                instr.then_inc(sems[sem], 1)
                cnt = t
            else:
                _, fn, inc, n = step
                instr = fn(eng)
                if inc is not None:
                    assert instr is not None
                    instr.then_inc(sems[inc[0]], n)


def build_nc():
    nc = bass.Bass("TRN2", target_bir_lowering=False)

    x_fm = nc.dram_tensor("x_fm", [D, T], F32, kind="ExternalInput").ap()
    x_mine = nc.dram_tensor("x_mine", [MT, D], F32, kind="ExternalInput").ap()
    wq_s = nc.dram_tensor("wq_s", [D, 2048], F32, kind="ExternalInput").ap()
    wk_s = nc.dram_tensor("wk_s", [D, KV], F32, kind="ExternalInput").ap()
    wv_s = nc.dram_tensor("wv_s", [D, KV], F32, kind="ExternalInput").ap()
    wo_t = nc.dram_tensor("wo_t", [1024, D], F32, kind="ExternalInput").ap()
    W_t = nc.dram_tensor("W_t", [D, D], F32, kind="ExternalInput").ap()
    b_r = nc.dram_tensor("b_r", [1, D], F32, kind="ExternalInput").ap()
    wqk_t = nc.dram_tensor("wqk_t", [D, 512], F32, kind="ExternalInput").ap()
    A_t = nc.dram_tensor("A_t", [D, 128], F32, kind="ExternalInput").ap()
    Bm_f = nc.dram_tensor("Bm_f", [128, D], F32, kind="ExternalInput").ap()
    msA_d = nc.dram_tensor("msA_d", [128, 1], F32, kind="ExternalInput").ap()
    msB_d = nc.dram_tensor("msB_d", [128, 1], F32, kind="ExternalInput").ap()
    out_d = nc.dram_tensor("out", [MT, D], F32, kind="ExternalOutput").ap()

    contrib = nc.dram_tensor("contrib", [MT, KV], BF16, kind="Internal")
    gout = nc.dram_tensor("gout", [2 * MT, KV], BF16, kind="Internal")
    x2f_d = nc.dram_tensor("x2f_d", [MT, D], F32, kind="Internal")
    x2b_d = nc.dram_tensor("x2b_d", [DC * MT, 128], BF16, kind="Internal")

    AREN = 207 * KB
    arena = nc.alloc_sbuf_tensor("arena", [128, AREN], U8)

    def vw(off_b, rows, cols, dt):
        sz = 2 if dt == BF16 else 4
        assert off_b + cols * sz <= AREN, (off_b, cols)
        return arena[0:rows, off_b:off_b + cols * sz].bitcast(dt)

    o = 0

    def nx(cols, dt, rows=128):
        nonlocal o
        ap = vw(o, rows, cols, dt)
        sz = 2 if dt == BF16 else 4
        o += max(32, ((cols * sz + 31) // 32) * 32)
        return ap

    ident_b = nx(128, BF16)
    ident_f = nx(128, F32)
    ones_col = nx(1, BF16)
    ones_row = nx(128, BF16, rows=1)
    c_n30 = nx(1, F32)
    c_eps = nx(1, F32)
    msA = nx(1, F32)
    msB = nx(1, F32)
    rstd = nx(NTT, F32)
    rstd_s = nx(NTT, F32)
    m2row = nx(GT, F32, rows=1)
    varrow = nx(GT, F32, rows=1)
    rstd_row = nx(GT, F32, rows=1)
    nmean2 = nx(NMT, F32)
    rstd2 = nx(NMT, F32)
    msum3 = nx(NMT, F32)
    vsum3 = nx(NMT, F32)
    mpart = nx(NMT, F32)
    vpart = nx(NMT, F32)
    nm3 = nx(NMT, F32)
    rstd3 = nx(NMT, F32)
    tmp1 = nx(1, F32)
    tmp2 = nx(1, F32)
    tmp3 = nx(8, F32)
    b4c = nx(1, F32)
    rstd4 = nx(1, F32)
    nm4 = nx(1, F32)
    rstd2t = nx(1, F32)
    var2t = nx(1, F32)
    rs_cols = nx(H * NTT, F32)
    mpart4 = nx(8, F32)
    nm_row = nx(T, BF16, rows=1)
    colsq = nx(KV, BF16, rows=1)
    colsk = nx(KV, BF16, rows=1)
    colsv = nx(KV, BF16, rows=1)
    gT_all = vw(24 * KB, 128, NMT * 128, BF16)
    Wb = [vw(204 * KB, 128, 512, BF16), vw(205 * KB, 128, 512, BF16)]
    fold_t = vw(204 * KB, 128, 512, F32)
    qtm = [nx(512, BF16), nx(512, BF16)]
    expT = [nx(512, BF16), nx(512, BF16)]
    vfm_st = [nx(512, BF16), nx(512, BF16)]
    se8 = nx(8, F32)
    we8 = nx(8, F32)
    wrep = nx(128, F32)
    gbf = nx(128, BF16)
    assert o <= 20 * KB, o
    prod = vw(20 * KB + 2 * KB, 128, 256, F32)

    junk = vw(20 * KB, 128, 4096, F32)
    lnt = vw(20 * KB, 128, 4096, F32)
    junk5 = vw(20 * KB, 128, 512, F32)
    jbf = vw(20 * KB, 128, 512, BF16)
    Bmb = vw(188 * KB, 128, D, BF16)
    wqc = vw(44 * KB, 128, DC * 512, BF16)
    vout = vw(44 * KB, 128, NTT * KV, BF16)
    attn_t = [vw(44 * KB, 128, 4096, F32), vw(60 * KB, 128, 4096, F32)]
    wqkb = vw(44 * KB, 128, DC * 512, BF16)
    ffn_sc = [vw(44 * KB, 128, D, BF16), vw(52 * KB, 128, D, BF16),
              vw(60 * KB, 128, D, BF16), vw(68 * KB, 128, D, BF16),
              vw(164 * KB, 128, D, BF16), vw(156 * KB, 128, D, BF16),
              vw(172 * KB, 128, D, BF16), vw(180 * KB, 128, D, BF16)]
    wkc = vw(76 * KB, 128, DC * 512, BF16)
    wvc = vw(108 * KB, 128, DC * 512, BF16)
    wob = vw(76 * KB, 128, 8 * D, BF16)
    x2cf = vw(76 * KB, 128, DC * MT, BF16)
    x2rd = [vw(76 * KB, 128, 4096, F32), vw(92 * KB, 128, 4096, F32),
            vw(108 * KB, 128, 4096, F32), vw(124 * KB, 128, 4096, F32)]
    xsq = vw(36 * KB, 128, (DC // 2) * GT, BF16)
    xcast = vw(140 * KB, 128, DC * GT, BF16)
    xcast2 = vw(20 * KB, 128, DC * GT, BF16)
    xcg = [xcast, xcast2]
    v_all = vw(140 * KB, 128, NMT * 1024, BF16)
    x2f_st = vw(140 * KB, 128, 4096, F32)
    lnt2 = vw(140 * KB, 128, 4096, F32)
    qfm = vw(172 * KB, 128, H * T, BF16)
    v_allT = vw(156 * KB, 128, 8 * MT, BF16)
    qk_all = vw(140 * KB, 128, NMT * 512, F32)
    h_tmp = [vw(160 * KB, 128, 128, F32), vw(161 * KB, 128, 128, F32)]
    wqraw = [vw(172 * KB, 128, 2048, F32), vw(180 * KB, 128, 2048, F32)]
    kfm = vw(156 * KB, 128, H * T, BF16)
    gbufA = vw(28 * KB, 128, NMT * 512, BF16)
    gbufB = vw(68 * KB, 128, NMT * 512, BF16)
    xmine_sb = vw(172 * KB, 128, 4096, F32)
    vtm = vw(188 * KB, 128, NTT * KV, BF16)
    contrib_st = vw(60 * KB, 128, NMT * 512, BF16)
    x2b_st = vw(188 * KB, 128, 4096, BF16)
    b_bf = vw(28 * KB, 1, D, BF16)
    Ab = vw(196 * KB, 128, DC * 128, BF16)
    Wb4 = [vw(36 * KB + i, 128, 1024, BF16) for i in range(0, 8 * KB, 2 * KB)]

    pbk = [nc.alloc_psum_tensor(f"pb{i}", [128, 2048], U8) for i in range(8)]

    def pv(bank, rows, cols, dt):
        sz = 2 if dt == BF16 else 4
        return pbk[bank][0:rows, 0:cols * sz].bitcast(dt)

    ps_rowf = pv(7, 128, 1, F32)
    ps_st = pv(7, 1, 512, F32)
    ps_cs = pv(6, 1, 512, F32)
    ps_q = [pv(0, 128, 512, F32), pv(1, 128, 512, F32)]
    ps_v = [pv(2, 128, 512, F32), pv(3, 128, 512, F32)]
    ps_k = [pv(4, 128, 256, F32), pv(5, 128, 256, F32)]
    ps_tr = pv(6, 128, 512, BF16)
    ps_s = [pv(0, 128, 512, F32), pv(1, 128, 512, F32)]
    ps_vv = [pv(2, 128, 512, F32), pv(3, 128, 512, F32)]
    ps_sum = [pv(4, 128, 4, F32), pv(5, 128, 4, F32)]
    ps_vt = pv(6, 128, 512, BF16)
    ps_va = [pv(2 + i, 128, 512, BF16) for i in range(4)]
    ps_wo = [pv(6, 128, 512, F32), pv(7, 128, 512, F32),
             pv(0, 128, 512, F32), pv(1, 128, 512, F32)]
    ps_h8 = [pv(i, 128, 128, F32) for i in range(8)]
    ps_g = pv(4, 128, 128, BF16)
    ps_f = [pv(i, 128, 512, F32) for i in range(8)]

    S = Sched()

    def VO(fn, label=None):
        S.op("vector", fn, label=label)

    def SO(fn, label=None):
        S.op("scalar", fn, label=label)

    S.op("gpsimd", lambda g: g.memset(ident_b[:, :], 0.0), inc=("s_gp", "idm"))
    S.wait("gpsimd", "s_gp", "idm")
    S.op("gpsimd", lambda g: g.affine_select(
        out=ident_b[:, :], in_=ident_b[:, :],
        compare_op=ALU.not_equal, fill=1.0, base=0,
        pattern=[[-1, 128]], channel_multiplier=1), inc=("s_gp", "identb"))
    S.op("gpsimd", lambda g: g.memset(ident_f[:, :], 0.0), inc=("s_gp", "idmf"))
    S.wait("gpsimd", "s_gp", "idmf")
    S.op("gpsimd", lambda g: g.affine_select(
        out=ident_f[:, :], in_=ident_f[:, :],
        compare_op=ALU.not_equal, fill=1.0, base=0,
        pattern=[[-1, 128]], channel_multiplier=1), inc=("s_gp", "ident"))
    S.op("gpsimd", lambda g: g.dma_start(out=msA[:, :], in_=msA_d), inc=("s_msk", "msA"), n=16)
    S.op("gpsimd", lambda g: g.dma_start(out=msB[:, :], in_=msB_d), inc=("s_msk", "msB"), n=16)
    def emit_xf(g_i):
        if g_i >= 2:
            S.wait("gpsimd", "s_se", f"kev{g_i - 2}_{H - 1}")
            S.wait("gpsimd", "s_pe", f"v{g_i - 2}_1")
        lo, hi = g_i * GT, (g_i + 1) * GT

        def g_xf(g, lo=lo, hi=hi, g_i=g_i):
            return g.dma_start(
                out=xcg[g_i % 2].rearrange("p (c t) -> p c t", t=GT),
                in_=x_fm[:, lo:hi].rearrange("(c p) t -> p c t", p=128))
        S.op("gpsimd", g_xf, inc=("s_xf" if g_i % 2 == 0 else "s_xg", f"xf{g_i}"), n=16)
    emit_xf(0)
    emit_xf(1)
    S.op("gpsimd", lambda g: g.dma_start(
        out=wkc.rearrange("p (c n) -> p c n", n=512),
        in_=wk_s.rearrange("(c p) n -> p c n", p=128)), inc=("s_wkv", "wk"), n=16)
    S.op("gpsimd", lambda g: g.dma_start(
        out=wvc.rearrange("p (c n) -> p c n", n=512),
        in_=wv_s.rearrange("(c p) n -> p c n", p=128)), inc=("s_wv", "wv"), n=16)
    for g_i in range(2, NG):
        emit_xf(g_i)
    S.wait("gpsimd", "s_pe", f"stat{NG - 1}")
    for idx in range(3):
        S.op("gpsimd", lambda g, og=0, cp=idx, idx=idx: g_w(g, og, cp, idx),
             inc=(f"s_w{idx % 4}", f"W{idx}"), n=16)
    S.wait("gpsimd", "s_se", f"kev{NG - 1}_{H - 1}")
    S.wait("gpsimd", "s_ve", f"vev{NG - 1}_1")
    S.op("gpsimd", lambda g: g.dma_start(
        out=wob.rearrange("p (a n) -> p a n", n=D),
        in_=wo_t.rearrange("(a p) n -> p a n", p=128)), inc=("s_wo", "wo"), n=16)
    HT2 = MT // 2
    S.wait("gpsimd", "s_ct", "contribA_store")
    S.op("gpsimd", lambda g: g.collective_compute(
        "AllGather", ALU.bypass,
        replica_groups=[[0, 1], [2, 3], [4, 5], [6, 7]],
        ins=[contrib.ap()[0:HT2, :]], outs=[gout.ap()[0:MT, :]]),
        inc=("s_cc", "agA"))
    S.wait("gpsimd", "s_ct", "contribB_store")
    S.op("gpsimd", lambda g: g.collective_compute(
        "AllGather", ALU.bypass,
        replica_groups=[[0, 1], [2, 3], [4, 5], [6, 7]],
        ins=[contrib.ap()[HT2:MT, :]], outs=[gout.ap()[MT:2 * MT, :]]),
        inc=("s_cc", "agB"))
    S.wait("gpsimd", "s_ve", f"xm_used{NMT - 1}")
    S.op("gpsimd", lambda g: g.dma_start(
        out=wqkb.rearrange("p (c n) -> p c n", n=512),
        in_=wqk_t.rearrange("(c p) n -> p c n", p=128)), inc=("s_qa", "wqk"), n=16)
    S.op("gpsimd", lambda g: g.dma_start(
        out=Ab.rearrange("p (c n) -> p c n", n=128),
        in_=A_t.rearrange("(c p) n -> p c n", p=128)), inc=("s_qa", "ab"), n=16)
    def g_w(g, og, cp, idx):
        return g.dma_start(
            out=Wb4[idx % 4].rearrange("p (c n) -> p c n", n=512),
            in_=W_t[cp * 256:(cp + 1) * 256,
                    og * 512:(og + 1) * 512].rearrange(
                        "(c p) n -> p c n", p=128))
    for og in range(8):
        for cp in range(DC // 2):
            idx = og * (DC // 2) + cp
            if idx < 3:
                continue
            if idx == 3:
                S.wait("gpsimd", "s_spf", val=16 * NMT)
                S.wait("gpsimd", "s_spb", val=16 * NMT)
                S.op("gpsimd", lambda g: g.dma_start(out=Bmb[:, :], in_=Bm_f),
                     inc=("s_bmb", "bm"), n=16)
            if idx >= 4:
                S.wait("gpsimd", "s_pe", f"Wp{idx - 4}")
            S.op("gpsimd", lambda g, og=og, cp=cp, idx=idx: g_w(g, og, cp, idx),
                 inc=(f"s_w{idx % 4}", f"W{idx}"), n=16)
            if idx == 3:
                S.op("gpsimd", lambda g: g.dma_start(out=b_bf[:, :], in_=b_r),
                     inc=("s_bmb", "b"), n=16)

    def s_wq(s, cc):
        return s.dma_start(out=wqraw[cc % 2][:, :], in_=wq_s[cc * 128:(cc + 1) * 128, :])
    for cc in range(DC):
        if cc >= 2:
            S.wait("sync", "s_ve", f"fold{cc - 2}")
        S.op("sync", lambda s, cc=cc: s_wq(s, cc),
             inc=("s_xa" if cc % 2 == 0 else "s_xb", f"wq{cc}"), n=16)
    HT = MT // 2
    S.wait("sync", "s_ve", "contribA_ready")
    S.op("sync", lambda s: s.dma_start(
        out=contrib.ap()[0:HT, :].rearrange("(j p) n -> p j n", p=128),
        in_=contrib_st[:, 0:4 * 512].rearrange("p (j n) -> p j n", n=512)),
        inc=("s_ct", "contribA_store"), n=16)
    S.wait("sync", "s_cc", val=1)
    S.op("sync", lambda s: s.dma_start(
        out=gbufA[:, 0:4 * 512].rearrange("p (j n) -> p j n", n=512),
        in_=gout.ap()[0:HT, :].rearrange("(j p) n -> p j n", p=128)),
        inc=("s_ga", "gA0"), n=16)
    S.op("sync", lambda s: s.dma_start(
        out=gbufB[:, 0:4 * 512].rearrange("p (j n) -> p j n", n=512),
        in_=gout.ap()[HT:MT, :].rearrange("(j p) n -> p j n", p=128)),
        inc=("s_ga", "gB0"), n=16)
    S.wait("sync", "s_ve", "contrib_ready")
    S.op("sync", lambda s: s.dma_start(
        out=contrib.ap()[HT:MT, :].rearrange("(j p) n -> p j n", p=128),
        in_=contrib_st[:, 4 * 512:8 * 512].rearrange("p (j n) -> p j n", n=512)),
        inc=("s_ct", "contribB_store"), n=16)
    S.wait("sync", "s_cc", val=2)
    S.op("sync", lambda s: s.dma_start(
        out=gbufA[:, 4 * 512:8 * 512].rearrange("p (j n) -> p j n", n=512),
        in_=gout.ap()[MT:MT + HT, :].rearrange("(j p) n -> p j n", p=128)),
        inc=("s_gb", "gA1"), n=16)
    S.op("sync", lambda s: s.dma_start(
        out=gbufB[:, 4 * 512:8 * 512].rearrange("p (j n) -> p j n", n=512),
        in_=gout.ap()[MT + HT:2 * MT, :].rearrange("(j p) n -> p j n", p=128)),
        inc=("s_gb", "gB1"), n=16)
    def s_xm(s, tt):
        return s.dma_start(out=xmine_sb[:, :], in_=x_mine[tt * 128:(tt + 1) * 128, :])
    def s_spf(s, tt):
        return s.dma_start(out=x2f_d.ap()[tt * 128:(tt + 1) * 128, :], in_=x2f_st[:, :])

    def s_spb(s, tt):
        return s.dma_start(
            out=x2b_d.ap().rearrange("(c g p) f -> g p c f", c=DC, p=128)[tt],
            in_=x2b_st.rearrange("p (c f) -> p c f", f=128))
    def s_tr_mk(cc):
        def s_tr(s, cc=cc):
            return s.dma_start_transpose(
                x2cf[:, cc * MT:(cc + 1) * MT],
                x2b_d.ap()[cc * MT:(cc + 1) * MT, :])
        return s_tr
    S.wait("sync", "s_spf", val=16 * NMT)
    S.wait("sync", "s_spb", val=16 * NMT)
    S.wait("sync", "s_ve", f"woev{NMT - 1}_7")
    for cc in range(DC):
        S.op("sync", s_tr_mk(cc), inc=(f"s_t{cc // 8}", f"tr{cc}"), n=16)
    for tt in range(NMT):
        if tt < 4:
            S.wait("sync", "s_pe", f"Wp{7 * (DC // 2) + 4 * tt + 3}")
        else:
            S.wait("sync", f"s_o{tt % 4}", f"st{tt - 4}")

        def s_rd(s, tt=tt):
            return s.dma_start(out=x2rd[tt % 4][:, :],
                               in_=x2f_d.ap()[tt * 128:(tt + 1) * 128, :])
        S.op("sync", s_rd, inc=("s_xa" if tt % 2 == 0 else "s_xb", f"x2rd{tt}"), n=16)
        S.wait("sync", "s_ve", f"out_ready{tt}")

        def s_st(s, tt=tt):
            return s.dma_start(out=out_d[tt * 128:(tt + 1) * 128, :], in_=x2rd[tt % 4][:, :])
        S.op("sync", s_st, inc=(f"s_o{tt % 4}", f"st{tt}"), n=16)
    for i in range(4):
        S.wait("sync", f"s_o{i}", val=16 * (NMT // 4))

    VO(lambda v: v.memset(ones_col[:, :], 1.0))
    VO(lambda v: v.memset(ones_row[:, :], 1.0))
    VO(lambda v: v.memset(c_n30[:, :], EXPB))
    VO(lambda v: v.memset(c_eps[:, :], EPS))
    VO(lambda v: v.memset(msum3[:, :], 0.0))
    VO(lambda v: v.memset(vsum3[:, :], 0.0), label="consts")
    for cc in range(DC):
        S.wait("vector", "s_xa" if cc % 2 == 0 else "s_xb", f"wq{cc}")

        def v_fold_r(v, cc=cc):
            src = wqraw[cc % 2].rearrange("p (k g d) -> p k d g", g=4, d=128)
            return v.tensor_reduce(
                fold_t.rearrange("p (k d) -> p k d", d=128),
                src, axis=AX.X, op=ALU.add)
        VO(v_fold_r)
        VO(lambda v, cc=cc: v.tensor_copy(wqc[:, cc * 512:(cc + 1) * 512], fold_t[:, :]),
           label=f"fold{cc}")
    S.lab[("s_ve", "wqc_done")] = S.tot["s_ve"]
    for g_i in range(NG):
        S.wait("vector", "s_se", f"m2r{g_i}")
        VO(lambda v, g_i=g_i: v.scalar_tensor_tensor(
            varrow[:, :], ps_st[:, GT:2 * GT], INV_D, m2row[:, :],
            ALU.mult, ALU.subtract), label=f"varr{g_i}")
        for t2 in range(2):
            tt = g_i * 2 + t2
            S.wait("vector", "s_se", f"rstdc{tt}")
            S.wait("vector", "s_pe", f"q{g_i}_{t2}")
            if g_i >= 1:
                S.wait("vector", "s_pe", f"qtr{g_i - 1}_{t2}")
            VO(lambda v, tt=tt, t2=t2: v.tensor_scalar(
                qtm[t2][:, :], ps_q[t2][:, :], rstd[:, tt:tt + 1], None, ALU.mult),
               label=f"qev{g_i}_{t2}")
            S.wait("vector", "s_pe", f"v{g_i}_{t2}")
            VO(lambda v, tt=tt, t2=t2: v.tensor_scalar(
                vtm[:, tt * 512:(tt + 1) * 512], ps_v[t2][:, :],
                rstd[:, tt:tt + 1], None, ALU.mult), label=f"vev{g_i}_{t2}")
    for idx in range(H * 4):
        r = idx % 2
        S.wait("vector", "s_pe", f"vs{idx}_15")
        if idx >= 2:
            S.wait("vector", "s_pe", f"vt{idx - 2}")
        VO(lambda v, r=r: v.tensor_copy(vfm_st[r][:, :], ps_vv[r][:, :]),
           label=f"vfm{idx}")
        VO(lambda v, idx=idx, r=r: v.reciprocal(
            rs_cols[:, idx * 4:idx * 4 + 4], ps_sum[r][:, :]), label=f"rs{idx}")
        S.wait("vector", "s_pe", f"vt{idx}")
        qg, h = idx // 4, idx % 4
        for sub in range(4):
            qt = qg * 4 + sub
            VO(lambda v, h=h, qt=qt, sub=sub, idx=idx: v.tensor_scalar(
                vout[:, qt * 512 + h * 128: qt * 512 + (h + 1) * 128],
                ps_vt[:, sub * 128:(sub + 1) * 128],
                rs_cols[:, idx * 4 + sub: idx * 4 + sub + 1], None, ALU.mult),
               label=(f"vout{idx}" if sub == 3 else None))
        if idx == 11:
            S.wait("vector", "s_msk", val=32)
            for j in range(4):
                VO(lambda v, j=j: v.tensor_scalar(
                    jbf[:, :], vout[:, (j + 8) * 512:(j + 9) * 512],
                    msA[:, :], None, ALU.mult))
                VO(lambda v, j=j: v.scalar_tensor_tensor(
                    contrib_st[:, j * 512:(j + 1) * 512],
                    vout[:, j * 512:(j + 1) * 512], msB[:, :],
                    jbf[:, :], ALU.mult, ALU.add),
                   label=("contribA_ready" if j == 3 else None))
    for j in range(4, NMT):
        VO(lambda v, j=j: v.tensor_scalar(
            jbf[:, :], vout[:, (j + 8) * 512:(j + 9) * 512], msA[:, :], None, ALU.mult))
        VO(lambda v, j=j: v.scalar_tensor_tensor(
            contrib_st[:, j * 512:(j + 1) * 512],
            vout[:, j * 512:(j + 1) * 512], msB[:, :],
            jbf[:, :], ALU.mult, ALU.add),
           label=("contrib_ready" if j == NMT - 1 else None))
    S.wait("vector", "s_ga", val=32)
    for j in range(NMT):
        if j == 4:
            S.wait("vector", "s_gb", val=32)
        VO(lambda v, j=j: v.tensor_scalar(
            jbf[:, :], vout[:, j * 512:(j + 1) * 512], msA[:, :], None, ALU.mult))
        VO(lambda v, j=j: v.scalar_tensor_tensor(
            v_all[:, j * 1024:j * 1024 + 512],
            gbufA[:, j * 512:(j + 1) * 512], msB[:, :],
            jbf[:, :], ALU.mult, ALU.add))
        VO(lambda v, j=j: v.tensor_scalar(
            jbf[:, :], vout[:, (j + 8) * 512:(j + 9) * 512], msB[:, :], None, ALU.mult))
        VO(lambda v, j=j: v.scalar_tensor_tensor(
            v_all[:, j * 1024 + 512:(j + 1) * 1024],
            gbufB[:, j * 512:(j + 1) * 512], msA[:, :],
            jbf[:, :], ALU.mult, ALU.add), label=f"va{j}")
    S.lab[("s_ve", "vall_all")] = S.tot["s_ve"]
    for tt in range(NMT):
        for og in range(8):
            S.wait("vector", "s_pe", f"wo{tt}_{og}")
            if og == 0 and tt >= 2:
                S.wait("vector", "s_se", f"ln4_{tt - 2}")
            VO(lambda v, tt=tt, og=og: v.tensor_scalar(
                attn_t[tt % 2][:, og * 512:(og + 1) * 512], ps_wo[og % 4][:, :],
                0.0, 0.0, ALU.add, ALU.add, accum_out=mpart4[:, og:og + 1]),
               label=f"woev{tt}_{og}")
        VO(lambda v: v.reduce_sum(tmp3[:, 6:7], mpart4[:, :], axis=AX.X),
           label=f"m4s_{tt}")
        S.wait("vector", "s_se", f"ln4_{tt}")
        S.wait("vector", "s_xm", f"xm{tt}")
        if tt >= 1:
            S.wait("vector", "s_spf", f"spf{tt - 1}")
        VO(lambda v, tt=tt: v.tensor_scalar(
            lnt[:, :], attn_t[tt % 2][:, :], nm4[:, :], rstd4[:, :], ALU.add, ALU.mult))
        VO(lambda v: v.scalar_tensor_tensor(
            x2f_st[:, :], lnt[:, :], 0.0, xmine_sb[:, :], ALU.add, ALU.add,
            accum_out=tmp3[:, 7:8]), label=f"xm_used{tt}")
        S.wait("vector", "s_se", f"r2_{tt}")
        if tt >= 1:
            S.wait("vector", "s_spb", f"spb{tt - 1}")
        VO(lambda v, tt=tt: v.tensor_scalar(
            x2b_st[:, :], x2f_st[:, :], nmean2[:, tt:tt + 1], rstd2[:, tt:tt + 1],
            ALU.add, ALU.mult), label=f"x2b_ready{tt}")
    for tt in range(NMT):
        S.wait("vector", "s_pe", f"qk{tt}")
        VO(lambda v, tt=tt: v.tensor_copy(qk_all[:, tt * 512:(tt + 1) * 512],
                                          ps_f[tt % 2][:, :]), label=f"qkev{tt}")
        S.wait("vector", "s_pe", f"hh{tt}")
        VO(lambda v, tt=tt: v.tensor_copy(h_tmp[tt % 2][:, :], ps_h8[2 + tt % 2][:, :]),
           label=f"hev{tt}")
        VO(lambda v, tt=tt: v.tensor_mul(prod[:, :], qk_all[:, tt * 512:tt * 512 + 256],
                                         qk_all[:, tt * 512 + 256:(tt + 1) * 512]))
        VO(lambda v: v.tensor_reduce(se8.rearrange("p (e u) -> p e u", u=1),
                                     prod.rearrange("p (e d) -> p e d", d=32),
                                     axis=AX.X, op=ALU.add))
        VO(lambda v: v.reduce_max(tmp1[:, :], se8[:, :], axis=AX.X))
        VO(lambda v: v.tensor_scalar(tmp2[:, :], tmp1[:, :], -ISQ_DK, None, ALU.mult),
           label=f"sm{tt}")
        S.wait("vector", "s_se", f"we{tt}")
        VO(lambda v: v.reduce_sum(tmp1[:, :], we8[:, :], axis=AX.X))
        VO(lambda v: v.reciprocal(tmp2[:, :], tmp1[:, :]))
        VO(lambda v: v.tensor_scalar(we8[:, :], we8[:, :], tmp2[:, :], None, ALU.mult))
        for rr in range(16):
            VO(lambda v, rr=rr: v.tensor_copy(
                wrep.rearrange("p (e r) -> p r e", r=16)[:, rr, :], we8[:, :]))
        if tt >= 1:
            S.wait("vector", "s_pe", f"gt{tt - 1}")
        VO(lambda v, tt=tt: v.tensor_mul(gbf[:, :], h_tmp[tt % 2][:, :], wrep[:, :]),
           label=f"g{tt}")
    S.lab[("s_ve", "p5a_done")] = S.tot["s_ve"]
    for og in range(8):
        for tt in range(NMT):
            S.wait("vector", "s_pe", f"f{og}_{tt}")
            VO(lambda v, og=og, tt=tt: v.tensor_copy(
                ffn_sc[tt][:, og * 512:(og + 1) * 512], ps_f[tt][:, :]))
            VO(lambda v, tt=tt: v.reduce_sum(mpart[:, tt:tt + 1], ps_f[tt][:, :],
                                             axis=AX.X))
            VO(lambda v, og=og, tt=tt: v.scalar_tensor_tensor(
                junk5[:, :], ffn_sc[tt][:, og * 512:(og + 1) * 512], 0.0,
                ffn_sc[tt][:, og * 512:(og + 1) * 512], ALU.add, ALU.mult,
                accum_out=vpart[:, tt:tt + 1]), label=f"fev{og}_{tt}")
        VO(lambda v: v.tensor_add(msum3[:, :], msum3[:, :], mpart[:, :]))
        VO(lambda v: v.tensor_add(vsum3[:, :], vsum3[:, :], vpart[:, :]),
           label=f"acc{og}")
    for tt in range(NMT):
        VO(lambda v, tt=tt: v.tensor_scalar(nm3[:, tt:tt + 1], msum3[:, tt:tt + 1],
                                            -INV_D, None, ALU.mult))
        VO(lambda v, tt=tt: v.tensor_scalar(tmp1[:, :], msum3[:, tt:tt + 1],
                                            INV_D, None, ALU.mult))
        VO(lambda v: v.tensor_mul(tmp2[:, :], tmp1[:, :], tmp1[:, :]))
        VO(lambda v, tt=tt: v.tensor_scalar(tmp1[:, :], vsum3[:, tt:tt + 1],
                                            INV_D, None, ALU.mult))
        VO(lambda v: v.tensor_sub(tmp3[:, 0:1], tmp1[:, :], tmp2[:, :]),
           label=f"f3_{tt}")
        S.wait("vector", "s_se", f"r3_{tt}")
        if tt == 0:
            S.wait("vector", "s_spf", val=16 * NMT)
            S.wait("vector", "s_spb", val=16 * NMT)
        VO(lambda v, tt=tt: v.tensor_scalar(
            lnt2[:, :], ffn_sc[tt][:, :], nm3[:, tt:tt + 1], rstd3[:, tt:tt + 1],
            ALU.add, ALU.mult))
        S.wait("vector", "s_xa" if tt % 2 == 0 else "s_xb", f"x2rd{tt}")
        VO(lambda v, tt=tt: v.tensor_add(x2rd[tt % 4][:, :], x2rd[tt % 4][:, :],
                                         lnt2[:, :]), label=f"out_ready{tt}")

    for g_i in range(NG):
        S.wait("scalar", "s_xf" if g_i % 2 == 0 else "s_xg", f"xf{g_i}")
        if g_i >= 1:
            S.wait("scalar", "s_pe", f"stat{g_i - 1}")
        SO(lambda s, g_i=g_i: s.activation(
            xsq[:, :], xcg[g_i % 2][:, 0:16 * GT], AF.Square), label=f"xsqr{g_i}_0")
        S.wait("scalar", "s_pe", f"sq0_{g_i}")
        SO(lambda s, g_i=g_i: s.activation(
            xsq[:, :], xcg[g_i % 2][:, 16 * GT:32 * GT], AF.Square),
           label=f"xsqr{g_i}_1")
        S.wait("scalar", "s_pe", f"stat{g_i}")
        SO(lambda s, g_i=g_i: s.activation(
            nm_row[:, g_i * GT:(g_i + 1) * GT], ps_st[:, 0:GT], AF.Copy, scale=-INV_D),
           label=f"nmr{g_i}")
        SO(lambda s: s.activation(m2row[:, :], ps_st[:, 0:GT], AF.Square, scale=INV_D),
           label=f"m2r{g_i}")
        if g_i == 0:
            S.wait("scalar", "s_pe", "csk")
            SO(lambda s: s.copy(colsk[:, :], ps_cs[:, :]), label="cske")
        for h in range(H):
            S.wait("scalar", "s_pe", f"k{g_i}_{h}")
            SO(lambda s, g_i=g_i, h=h: s.copy(
                kfm[:, h * T + g_i * GT: h * T + (g_i + 1) * GT], ps_k[h % 2][:, :]),
               label=f"kev{g_i}_{h}")
        S.wait("scalar", "s_ve", f"varr{g_i}")
        SO(lambda s: s.activation(varrow[:, :], varrow[:, :], AF.Ln,
                                  bias=c_eps[0:1, :], scale=1.0))
        SO(lambda s: s.activation(rstd_row[:, :], varrow[:, :], AF.Exp, scale=-0.5),
           label=f"rstdr{g_i}")
        for t2 in range(2):
            tt = g_i * 2 + t2
            S.wait("scalar", "s_pe", f"rtr{tt}")
            SO(lambda s, tt=tt: s.copy(rstd[:, tt:tt + 1], ps_rowf[:, :]),
               label=f"rstdc{tt}")
        SO(lambda s, g_i=g_i: s.activation(
            rstd_s[:, g_i * 2:g_i * 2 + 2], rstd[:, g_i * 2:g_i * 2 + 2],
            AF.Copy, scale=ISQ_HD), label=f"rstds{g_i}")
        if g_i == 0:
            S.wait("scalar", "s_pe", "csv")
            SO(lambda s: s.copy(colsv[:, :], ps_cs[:, :]), label="csve")
            S.wait("scalar", "s_pe", "csq")
            SO(lambda s: s.copy(colsq[:, :], ps_cs[:, :]), label="csqe")
        if g_i == 0:
            S.wait("scalar", "s_ve", "wqc_done")
        for t2 in range(2):
            S.wait("scalar", "s_pe", f"qtr{g_i}_{t2}")
            tt = g_i * 2 + t2
            for h in range(H):
                SO(lambda s, tt=tt, h=h: s.copy(
                    qfm[:, h * T + tt * 128: h * T + (tt + 1) * 128],
                    ps_tr[:, h * 128:(h + 1) * 128]),
                   label=(f"qtrc{g_i}_{t2}" if h == H - 1 else None))
    for idx in range(H * 4):
        for kt in range(NTT):
            S.wait("scalar", "s_pe", f"s{idx}_{kt}")
            if kt >= 2:
                S.wait("scalar", "s_pe", f"vs{idx}_{kt - 2}")
            elif idx > 0:
                S.wait("scalar", "s_pe", f"vs{idx - 1}_{14 + kt}")
            SO(lambda s, kt=kt: s.activation(
                expT[kt % 2][:, :], ps_s[kt % 2][:, :], AF.Exp,
                bias=c_n30[:, :], scale=rstd_s[:, kt:kt + 1]), label=f"e{idx}_{kt}")
    for j in range(NMT):
        for ag in range(2):
            vi = j * 2 + ag
            S.wait("scalar", "s_pe", f"vat{j}_{ag}")
            for k in range(4):
                a = ag * 4 + k
                SO(lambda s, j=j, a=a, k=k, vi=vi: s.copy(
                    v_allT[:, a * MT + j * 128: a * MT + (j + 1) * 128],
                    ps_va[vi % 4][:, k * 128:(k + 1) * 128]),
                   label=(f"vac{j}_{ag}" if k == 3 else None))
    for tt in range(NMT):
        S.wait("scalar", "s_ve", f"m4s_{tt}")
        SO(lambda s, tt=tt: s.activation(junk[:, :], attn_t[tt % 2][:, :], AF.Square,
                                         accum_out=tmp3[:, 2:3]))
        SO(lambda s: s.activation(nm4[:, :], tmp3[:, 6:7], AF.Copy, scale=-INV_D))
        SO(lambda s: s.activation(b4c[:, :], nm4[:, :], AF.Square))
        SO(lambda s: s.activation(b4c[:, :], b4c[:, :], AF.Copy, bias=EPS, scale=-1.0))
        SO(lambda s: s.activation(tmp3[:, 3:4], tmp3[:, 2:3], AF.Ln,
                                  bias=b4c[:, :], scale=INV_D))
        SO(lambda s: s.activation(rstd4[:, :], tmp3[:, 3:4], AF.Exp, scale=-0.5),
           label=f"ln4_{tt}")
        if tt == 0:
            S.wait("scalar", "s_ve", "vall_all")
            S.op("scalar", lambda s: s_xm(s, 0), inc=("s_xm", "xm0"), n=16, dma=True)
        S.wait("scalar", "s_ve", f"xm_used{tt}")
        if tt + 1 < NMT:
            S.op("scalar", lambda s, tt=tt: s_xm(s, tt + 1),
                 inc=("s_xm", f"xm{tt + 1}"), n=16, dma=True)
        S.op("scalar", lambda s, tt=tt: s_spf(s, tt), inc=("s_spf", f"spf{tt}"),
             n=16, dma=True)
        SO(lambda s: s.activation(junk[:, :], x2f_st[:, :], AF.Square,
                                  accum_out=tmp3[:, 4:5]))
        SO(lambda s, tt=tt: s.activation(nmean2[:, tt:tt + 1], tmp3[:, 7:8], AF.Copy,
                                         scale=-INV_D))
        SO(lambda s, tt=tt: s.activation(var2t[:, :], nmean2[:, tt:tt + 1], AF.Square))
        SO(lambda s: s.activation(var2t[:, :], var2t[:, :], AF.Copy, bias=EPS, scale=-1.0))
        SO(lambda s: s.activation(rstd2t[:, :], tmp3[:, 4:5], AF.Ln,
                                  bias=var2t[:, :], scale=INV_D))
        SO(lambda s, tt=tt: s.activation(rstd2[:, tt:tt + 1], rstd2t[:, :], AF.Exp,
                                         scale=-0.5), label=f"r2_{tt}")
        S.wait("scalar", "s_ve", f"x2b_ready{tt}")
        S.op("scalar", lambda s, tt=tt: s_spb(s, tt), inc=("s_spb", f"spb{tt}"),
             n=16, dma=True)
    for tt in range(NMT):
        S.wait("scalar", "s_ve", f"sm{tt}")
        SO(lambda s: s.activation(we8[:, :], se8[:, :], AF.Exp,
                                  bias=tmp2[:, :], scale=ISQ_DK), label=f"we{tt}")
        S.wait("scalar", "s_pe", f"gt{tt}")
        SO(lambda s, tt=tt: s.copy(gT_all[:, tt * 128:(tt + 1) * 128], ps_g[:, :]),
           label=f"gtc{tt}")
    for tt in range(NMT):
        S.wait("scalar", "s_ve", f"f3_{tt}")
        SO(lambda s: s.activation(tmp3[:, 5:6], tmp3[:, 0:1], AF.Ln,
                                  bias=c_eps[:, :], scale=1.0))
        SO(lambda s, tt=tt: s.activation(rstd3[:, tt:tt + 1], tmp3[:, 5:6], AF.Exp,
                                         scale=-0.5), label=f"r3_{tt}")

    S.wait("tensor", "s_gp", "ident")
    S.wait("tensor", "s_ve", "consts")

    def emit_qtr(g_x):
        for t2 in range(2):
            S.wait("tensor", "s_ve", f"qev{g_x}_{t2}")
            if g_x == 0 and t2 == 0:
                S.wait("tensor", "s_se", "csqe")
            if not (g_x == 0 and t2 == 0):
                pg, pt = (g_x, 0) if t2 == 1 else (g_x - 1, 1)
                S.wait("tensor", "s_se", f"qtrc{pg}_{pt}")

            def t_qtr(t, t2=t2):
                ins = None
                for h in range(H):
                    ins = t.transpose(ps_tr[:, h * 128:(h + 1) * 128],
                                      qtm[t2][:, h * 128:(h + 1) * 128],
                                      ident_b[:, :])
                return ins
            S.op("tensor", t_qtr, inc=("s_pe", f"qtr{g_x}_{t2}"))

    for g_i in range(NG):
        if g_i >= 1:
            emit_qtr(g_i - 1)
        S.wait("tensor", "s_xf" if g_i % 2 == 0 else "s_xg", f"xf{g_i}")
        if g_i >= 1:
            S.wait("tensor", "s_se", f"rstdc{2 * g_i - 1}")

        def t_sum(t, g_i=g_i):
            xc = xcg[g_i % 2]
            ins = None
            for cc in range(DC):
                ins = t.matmul(ps_st[:, 0:GT], ones_col[:, :],
                               xc[:, cc * GT:(cc + 1) * GT],
                               start=(cc == 0), stop=(cc == DC - 1))
            return ins
        S.op("tensor", t_sum, inc=("s_pe", f"sums{g_i}"))
        S.wait("tensor", "s_se", f"xsqr{g_i}_0")

        def t_sq0(t):
            ins = None
            for cc in range(16):
                ins = t.matmul(ps_st[:, GT:2 * GT], ones_col[:, :],
                               xsq[:, cc * GT:(cc + 1) * GT],
                               start=(cc == 0), stop=False)
            return ins
        S.op("tensor", t_sq0, inc=("s_pe", f"sq0_{g_i}"))
        S.wait("tensor", "s_se", f"xsqr{g_i}_1")

        def t_sq1(t):
            ins = None
            for cc in range(16):
                ins = t.matmul(ps_st[:, GT:2 * GT], ones_col[:, :],
                               xsq[:, cc * GT:(cc + 1) * GT],
                               start=False, stop=(cc == 15))
            return ins
        S.op("tensor", t_sq1, inc=("s_pe", f"stat{g_i}"))

        def t_cs_mk(w):
            def t_cs(t, w=w):
                ins = None
                for cc in range(DC):
                    ins = t.matmul(ps_cs[:, :], ones_col[:, :],
                                   w[:, cc * 512:(cc + 1) * 512],
                                   start=(cc == 0), stop=(cc == DC - 1))
                return ins
            return t_cs
        if g_i == 0:
            S.wait("tensor", "s_wkv", val=16)
            S.op("tensor", t_cs_mk(wkc), inc=("s_pe", "csk"))
            S.wait("tensor", "s_se", "cske")
        S.wait("tensor", "s_se", f"nmr{g_i}")
        for h in range(H):
            if g_i >= 1 or h >= 2:
                ppg, pph = (g_i, h - 2) if h >= 2 else (g_i - 1, h + 2)
                S.wait("tensor", "s_se", f"kev{ppg}_{pph}")

            def t_k(t, g_i=g_i, h=h):
                xc = xcg[g_i % 2]
                for cc in range(DC):
                    t.matmul(ps_k[h % 2][:, :],
                             wkc[:, cc * 512 + h * 128: cc * 512 + (h + 1) * 128],
                             xc[:, cc * GT:(cc + 1) * GT],
                             start=(cc == 0), stop=False)
                return t.matmul(ps_k[h % 2][:, :],
                                colsk[:, h * 128:(h + 1) * 128],
                                nm_row[:, g_i * GT:(g_i + 1) * GT],
                                start=False, stop=True)
            S.op("tensor", t_k, inc=("s_pe", f"k{g_i}_{h}"))
        S.wait("tensor", "s_se", f"rstdr{g_i}")
        for t2 in range(2):
            tt = g_i * 2 + t2
            if tt >= 1:
                S.wait("tensor", "s_se", f"rstdc{tt - 1}")

            def t_rtr(t, t2=t2):
                return t.transpose(ps_rowf[:, :],
                                   rstd_row[:, t2 * 128:(t2 + 1) * 128],
                                   ident_f[0:1, 0:1])
            S.op("tensor", t_rtr, inc=("s_pe", f"rtr{tt}"))
        if g_i == 0:
            S.wait("tensor", "s_wv", val=16)
            S.op("tensor", t_cs_mk(wvc), inc=("s_pe", "csv"))
            S.wait("tensor", "s_ve", "wqc_done")
            S.wait("tensor", "s_se", "csve")
            S.op("tensor", t_cs_mk(wqc), inc=("s_pe", "csq"))
            S.wait("tensor", "s_se", "csqe")
        S.wait("tensor", "s_se", f"rstds{g_i}")
        for t2 in range(2):
            tt = g_i * 2 + t2
            if g_i >= 1:
                S.wait("tensor", "s_ve", f"qev{g_i - 1}_{t2}")

            def t_q(t, t2=t2, tt=tt, g_i=g_i):
                xc = xcg[g_i % 2]
                for cc in range(DC):
                    t.matmul(ps_q[t2][:, :],
                             xc[:, cc * GT + t2 * 128: cc * GT + (t2 + 1) * 128],
                             wqc[:, cc * 512:(cc + 1) * 512],
                             start=(cc == 0), stop=False)
                return t.matmul(ps_q[t2][:, :],
                                nm_row[:, tt * 128:(tt + 1) * 128], colsq[:, :],
                                start=False, stop=True)
            S.op("tensor", t_q, inc=("s_pe", f"q{g_i}_{t2}"))
            if g_i >= 1:
                S.wait("tensor", "s_ve", f"vev{g_i - 1}_{t2}")

            def t_v(t, t2=t2, tt=tt, g_i=g_i):
                xc = xcg[g_i % 2]
                for cc in range(DC):
                    t.matmul(ps_v[t2][:, :],
                             xc[:, cc * GT + t2 * 128: cc * GT + (t2 + 1) * 128],
                             wvc[:, cc * 512:(cc + 1) * 512],
                             start=(cc == 0), stop=False)
                return t.matmul(ps_v[t2][:, :],
                                nm_row[:, tt * 128:(tt + 1) * 128], colsv[:, :],
                                start=False, stop=True)
            S.op("tensor", t_v, inc=("s_pe", f"v{g_i}_{t2}"))

    emit_qtr(NG - 1)

    def emit_vtr(idx):
        S.wait("tensor", "s_ve", f"vfm{idx}")
        if idx >= 1:
            S.wait("tensor", "s_ve", f"vout{idx - 1}")

        def t_vtr(t, idx=idx):
            r = idx % 2
            ins = None
            for sub in range(4):
                ins = t.transpose(ps_vt[:, sub * 128:(sub + 1) * 128],
                                  vfm_st[r][:, sub * 128:(sub + 1) * 128],
                                  ident_b[:, :])
            return ins
        S.op("tensor", t_vtr, inc=("s_pe", f"vt{idx}"))

    S.wait("tensor", "s_se", f"qtrc{NG - 1}_1")
    S.wait("tensor", "s_se", f"kev{NG - 1}_{H - 1}")
    S.wait("tensor", "s_ve", f"vev{NG - 1}_1")
    prev_idx = None
    for qg in range(4):
        for h in range(H):
            idx = qg * 4 + h
            r = idx % 2
            def emit_vs(kt, h=h, r=r, idx=idx):
                S.wait("tensor", "s_se", f"e{idx}_{kt}")
                if kt == 0 and idx >= 2:
                    S.wait("tensor", "s_ve", f"vfm{idx - 2}")
                    S.wait("tensor", "s_ve", f"rs{idx - 2}")

                def t_vs(t, h=h, kt=kt, r=r):
                    t.matmul(ps_vv[r][:, :],
                             vtm[:, kt * 512 + h * 128: kt * 512 + (h + 1) * 128],
                             expT[kt % 2][:, :], start=(kt == 0), stop=(kt == NTT - 1))
                    ins = None
                    for sub in range(4):
                        ins = t.matmul(ps_sum[r][:, sub:sub + 1],
                                       expT[kt % 2][:, sub * 128:(sub + 1) * 128],
                                       ones_col[:, :],
                                       start=(kt == 0 and sub == 0),
                                       stop=(kt == NTT - 1 and sub == 3))
                    return ins
                S.op("tensor", t_vs, inc=("s_pe", f"vs{idx}_{kt}"))

            for kt in range(NTT):
                if kt >= 2:
                    S.wait("tensor", "s_se", f"e{idx}_{kt - 2}")
                elif idx >= 1:
                    S.wait("tensor", "s_se", f"e{idx - 1}_{14 + kt}")

                def t_s(t, h=h, qg=qg, kt=kt):
                    return t.matmul(ps_s[kt % 2][:, :],
                                    kfm[:, h * T + kt * 128: h * T + (kt + 1) * 128],
                                    qfm[:, h * T + qg * 512: h * T + (qg + 1) * 512],
                                    start=True, stop=True)
                S.op("tensor", t_s, inc=("s_pe", f"s{idx}_{kt}"))
                if kt >= 1:
                    emit_vs(kt - 1)
            emit_vs(NTT - 1)
            if prev_idx is not None:
                emit_vtr(prev_idx)
            prev_idx = idx
    emit_vtr(prev_idx)
    S.wait("tensor", "s_ve", f"vout{H * 4 - 1}")
    S.wait("tensor", "s_se", f"e{H * 4 - 1}_{NTT - 1}")
    for j in range(NMT):
        S.wait("tensor", "s_ve", f"va{j}")
        for ag in range(2):
            vi = j * 2 + ag
            if vi >= 4:
                p = vi - 4
                S.wait("tensor", "s_se", f"vac{p // 2}_{p % 2}")

            def t_vat(t, j=j, ag=ag, vi=vi):
                ins = None
                for k in range(4):
                    a = ag * 4 + k
                    ins = t.transpose(ps_va[vi % 4][:, k * 128:(k + 1) * 128],
                                      v_all[:, j * 1024 + a * 128: j * 1024 + (a + 1) * 128],
                                      ident_b[:, :])
                return ins
            S.op("tensor", t_vat, inc=("s_pe", f"vat{j}_{ag}"))
    S.wait("tensor", "s_wo", val=16)
    S.wait("tensor", "s_se", f"vac{NMT - 1}_1")
    S.wait("tensor", "s_se", f"e{H * 4 - 1}_{NTT - 1}")
    for tt in range(NMT):
        for og in range(8):
            if tt * 8 + og >= 4:
                p = tt * 8 + og - 4
                S.wait("tensor", "s_ve", f"woev{p // 8}_{p % 8}")

            def t_wo(t, tt=tt, og=og):
                ins = None
                for a in range(8):
                    ins = t.matmul(ps_wo[og % 4][:, :],
                                   v_allT[:, a * MT + tt * 128: a * MT + (tt + 1) * 128],
                                   wob[:, a * D + og * 512: a * D + (og + 1) * 512],
                                   start=(a == 0), stop=(a == 7))
                return ins
            S.op("tensor", t_wo, inc=("s_pe", f"wo{tt}_{og}"))
    def emit_gt(tt):
        S.wait("tensor", "s_ve", f"g{tt}")
        if tt >= 1:
            S.wait("tensor", "s_se", f"gtc{tt - 1}")

        def t_gt(t):
            return t.transpose(ps_g[:, :], gbf[:, :], ident_b[:, :])
        S.op("tensor", t_gt, inc=("s_pe", f"gt{tt}"))

    S.wait("tensor", "s_qa", val=32)
    for i in range(4):
        S.wait("tensor", f"s_t{i}", val=16 * 8)
    for tt in range(NMT):
        if tt >= 2:
            S.wait("tensor", "s_ve", f"qkev{tt - 2}")

        def t_qk(t, tt=tt):
            ins = None
            for cc in range(DC):
                ins = t.matmul(ps_f[tt % 2][:, :],
                               x2cf[:, cc * MT + tt * 128: cc * MT + (tt + 1) * 128],
                               wqkb[:, cc * 512:(cc + 1) * 512],
                               start=(cc == 0), stop=(cc == DC - 1))
            return ins
        S.op("tensor", t_qk, inc=("s_pe", f"qk{tt}"))
        if tt >= 2:
            S.wait("tensor", "s_ve", f"hev{tt - 2}")

        def t_h(t, tt=tt):
            ins = None
            for cc in range(DC):
                ins = t.matmul(ps_h8[2 + tt % 2][:, :],
                               x2cf[:, cc * MT + tt * 128: cc * MT + (tt + 1) * 128],
                               Ab[:, cc * 128:(cc + 1) * 128],
                               start=(cc == 0), stop=(cc == DC - 1))
            return ins
        S.op("tensor", t_h, inc=("s_pe", f"hh{tt}"))
        if tt >= 1:
            emit_gt(tt - 1)
    emit_gt(NMT - 1)
    S.wait("tensor", "s_bmb", val=32)
    S.wait("tensor", "s_ve", "p5a_done")
    S.wait("tensor", "s_se", f"gtc{NMT - 1}")
    for og in range(8):
        for cp in range(DC // 2):
            idx = og * (DC // 2) + cp
            S.wait("tensor", f"s_w{idx % 4}", f"W{idx}")
            if cp == 0 and og >= 1:
                for tt in range(NMT):
                    S.wait("tensor", "s_ve", f"fev{og - 1}_{tt}")

            if cp < DC // 2 - 1:
                def t_w(t, cp=cp, idx=idx):
                    ins = None
                    for half in range(2):
                        cc = cp * 2 + half
                        for tt in range(NMT):
                            ins = t.matmul(
                                ps_f[tt][:, :],
                                x2cf[:, cc * MT + tt * 128: cc * MT + (tt + 1) * 128],
                                Wb4[idx % 4][:, half * 512:(half + 1) * 512],
                                start=(cc == 0), stop=False)
                    return ins
                S.op("tensor", t_w, inc=("s_pe", f"Wp{idx}"))
            else:
                def t_w30(t, cp=cp, idx=idx):
                    cc = cp * 2
                    ins = None
                    for tt in range(NMT):
                        ins = t.matmul(
                            ps_f[tt][:, :],
                            x2cf[:, cc * MT + tt * 128: cc * MT + (tt + 1) * 128],
                            Wb4[idx % 4][:, 0:512], start=False, stop=False)
                    return ins
                S.op("tensor", t_w30, inc=("s_pe", f"Wc30_{og}"))
                for tt in range(NMT):
                    def t_wt(t, cp=cp, idx=idx, og=og, tt=tt):
                        cc = cp * 2 + 1
                        t.matmul(ps_f[tt][:, :],
                                 x2cf[:, cc * MT + tt * 128: cc * MT + (tt + 1) * 128],
                                 Wb4[idx % 4][:, 512:1024], start=False, stop=False)
                        t.matmul(ps_f[tt][:, :], ones_row[:, :],
                                 b_bf[:, og * 512:(og + 1) * 512],
                                 start=False, stop=False)
                        return t.matmul(ps_f[tt][:, :], gT_all[:, tt * 128:(tt + 1) * 128],
                                        Bmb[:, og * 512:(og + 1) * 512],
                                        start=False, stop=True)
                    if tt < NMT - 1:
                        S.op("tensor", t_wt, inc=("s_pe", f"f{og}_{tt}"))
                    else:
                        S.op("tensor", t_wt, inc=("s_pe", f"Wp{idx}"))
                        S.lab[("s_pe", f"f{og}_{NMT - 1}")] = S.tot["s_pe"]

    sem_names = ["s_xa", "s_xb", "s_xm", "s_ct", "s_ga", "s_msk", "s_wkv", "s_wv", "s_gb",
                 "s_bmb", "s_wo", "s_qa", "s_xf",
                 "s_spf", "s_spb", "s_oa", "s_ob", "s_pe", "s_ve", "s_se",
                 "s_gp", "s_cc", "s_tr", "s_xg"] + [f"s_w{i}" for i in range(8)]
    sem_names += [f"s_t{i}" for i in range(4)]
    sem_names = [s for s in sem_names if s not in ("s_oa", "s_ob")] + [f"s_o{i}" for i in range(4)]
    import contextlib
    with contextlib.ExitStack() as stk:
        sems = {n: stk.enter_context(nc.semaphore(n)) for n in sem_names}
        block = stk.enter_context(nc.Block())
        block.sync(lambda e: S.emit("sync", e, sems))
        block.gpsimd(lambda e: S.emit("gpsimd", e, sems))
        block.tensor(lambda e: S.emit("tensor", e, sems))
        block.vector(lambda e: S.emit("vector", e, sems))
        block.scalar(lambda e: S.emit("scalar", e, sems))
    return nc


_NC_CACHE = None


def kernel(L0_attn_x, L0_attn_wq, L0_attn_wk, L0_attn_wv, L0_attn_wo,
           L0_ffn_W, L0_ffn_b, L0_ffn_Wq, L0_ffn_Wk, L0_ffn_A, L0_ffn_Bm):
    global _NC_CACHE
    x = np.asarray(L0_attn_x, dtype=np.float32)
    wq = np.asarray(L0_attn_wq, np.float32)
    wk = np.asarray(L0_attn_wk, np.float32)
    wv = np.asarray(L0_attn_wv, np.float32)
    wo = np.asarray(L0_attn_wo, np.float32)
    W = np.asarray(L0_ffn_W, np.float32)
    b = np.asarray(L0_ffn_b, np.float32)
    Wq = np.asarray(L0_ffn_Wq, np.float32)
    Wk = np.asarray(L0_ffn_Wk, np.float32)
    A = np.asarray(L0_ffn_A, np.float32)
    Bm = np.asarray(L0_ffn_Bm, np.float32)

    W_t = np.ascontiguousarray(W.T)
    wqk_t = np.ascontiguousarray(np.concatenate([Wq, Wk], axis=0).T)
    A_t = np.ascontiguousarray(A.reshape(128, D).T)
    Bm_f = np.ascontiguousarray(Bm.transpose(0, 2, 1).reshape(128, D))
    b_rs = np.ascontiguousarray(b.reshape(1, D))
    wo_c = np.ascontiguousarray(wo)

    in_maps = []
    for c in range(8):
        bi, hi = c // 2, c % 2
        in_maps.append({
            "x_tm": np.ascontiguousarray(x[bi]),
            "x_fm": np.ascontiguousarray(x[bi].T),
            "x_mine": np.ascontiguousarray(x[bi, hi * MT:(hi + 1) * MT]),
            "wq_s": np.ascontiguousarray(wq[:, hi * 2048:(hi + 1) * 2048]),
            "wk_s": np.ascontiguousarray(wk[:, hi * KV:(hi + 1) * KV]),
            "wv_s": np.ascontiguousarray(wv[:, hi * KV:(hi + 1) * KV]),
            "wo_t": wo_c,
            "W_t": W_t,
            "b_r": b_rs,
            "wqk_t": wqk_t,
            "A_t": A_t,
            "Bm_f": Bm_f,
            "msA_d": np.full((128, 1), 1.0 - hi, np.float32),
            "msB_d": np.full((128, 1), float(hi), np.float32),
        })
    if _NC_CACHE is None:
        _NC_CACHE = build_nc()
    res = run_bass_kernel_spmd(_NC_CACHE, in_maps, core_ids=list(range(8)))
    out = np.empty((4, T, D), np.float32)
    for c in range(8):
        bi, hi = c // 2, c % 2
        out[bi, hi * MT:(hi + 1) * MT] = res.results[c]["out"]
    return out


# version 17 (final)
# speedup vs baseline: 1.0005x; 1.0005x over previous
import numpy as np
import ml_dtypes

import concourse.bass as bass
import concourse.mybir as mybir
from concourse.bass_utils import run_bass_kernel_spmd

F32 = mybir.dt.float32
BF16 = mybir.dt.bfloat16
U8 = mybir.dt.uint8
AF = mybir.ActivationFunctionType
ALU = mybir.AluOpType
AX = mybir.AxisListType

T = 2048
D = 4096
MT = 1024
NTT = 16
NMT = 8
DC = 32
H = 4
KV = 512
NG = 8
GT = 256
ISQ_HD = float(1.0 / np.sqrt(128.0))
ISQ_DK = float(1.0 / np.sqrt(32.0))
INV_D = float(1.0 / 4096.0)
EPS = 1e-5
EXPB = -30.0
KB = 1024

CHAIN_SEM = {"vector": "s_ve", "scalar": "s_se"}


class Sched:
    ENGINES = ("sync", "gpsimd", "tensor", "vector", "scalar")

    def __init__(self):
        self.steps = {k: [] for k in self.ENGINES}
        self.tot = {}
        self.lab = {}

    def op(self, eng, fn, label=None, inc=None, n=1, dma=False):
        if eng in CHAIN_SEM and not dma:
            assert inc is None
            sem = CHAIN_SEM[eng]
            t = self.tot.get(sem, 0) + 1
            self.tot[sem] = t
            if label is not None:
                assert (sem, label) not in self.lab
                self.lab[(sem, label)] = t
            self.steps[eng].append(("cop", fn, sem, t))
        else:
            if inc is not None:
                sem, lbl = inc
                t = self.tot.get(sem, 0) + n
                self.tot[sem] = t
                assert (sem, lbl) not in self.lab, (sem, lbl)
                self.lab[(sem, lbl)] = t
            self.steps[eng].append(("op", fn, inc, n))

    def wait(self, eng, sem, label=None, val=None):
        assert (label is None) != (val is None)
        self.steps[eng].append(("wait", sem, label, val))

    def emit(self, eng_name, eng, sems):
        cnt = 0
        for step in self.steps[eng_name]:
            if step[0] == "wait":
                _, sem, label, val = step
                v = val if val is not None else self.lab[(sem, label)]
                eng.wait_ge(sems[sem], v)
            elif step[0] == "cop":
                _, fn, sem, t = step
                if cnt > 0:
                    eng.wait_ge(sems[sem], cnt)
                instr = fn(eng)
                assert instr is not None
                instr.then_inc(sems[sem], 1)
                cnt = t
            else:
                _, fn, inc, n = step
                instr = fn(eng)
                if inc is not None:
                    assert instr is not None
                    instr.then_inc(sems[inc[0]], n)


def build_nc():
    nc = bass.Bass("TRN2", target_bir_lowering=False)

    x_fm = nc.dram_tensor("x_fm", [D, T], F32, kind="ExternalInput").ap()
    x_mine = nc.dram_tensor("x_mine", [MT, D], F32, kind="ExternalInput").ap()
    wq_s = nc.dram_tensor("wq_s", [D, 2048], F32, kind="ExternalInput").ap()
    wk_s = nc.dram_tensor("wk_s", [D, KV], F32, kind="ExternalInput").ap()
    wv_s = nc.dram_tensor("wv_s", [D, KV], F32, kind="ExternalInput").ap()
    wo_t = nc.dram_tensor("wo_t", [1024, D], F32, kind="ExternalInput").ap()
    W_t = nc.dram_tensor("W_t", [D, D], F32, kind="ExternalInput").ap()
    b_r = nc.dram_tensor("b_r", [1, D], F32, kind="ExternalInput").ap()
    wqk_t = nc.dram_tensor("wqk_t", [D, 512], F32, kind="ExternalInput").ap()
    A_t = nc.dram_tensor("A_t", [D, 128], F32, kind="ExternalInput").ap()
    Bm_f = nc.dram_tensor("Bm_f", [128, D], F32, kind="ExternalInput").ap()
    msA_d = nc.dram_tensor("msA_d", [128, 1], F32, kind="ExternalInput").ap()
    msB_d = nc.dram_tensor("msB_d", [128, 1], F32, kind="ExternalInput").ap()
    out_d = nc.dram_tensor("out", [MT, D], F32, kind="ExternalOutput").ap()

    contrib = nc.dram_tensor("contrib", [MT, KV], BF16, kind="Internal")
    gout = nc.dram_tensor("gout", [2 * MT, KV], BF16, kind="Internal")
    x2f_d = nc.dram_tensor("x2f_d", [MT, D], F32, kind="Internal")
    x2b_d = nc.dram_tensor("x2b_d", [DC * MT, 128], BF16, kind="Internal")

    AREN = 207 * KB
    arena = nc.alloc_sbuf_tensor("arena", [128, AREN], U8)

    def vw(off_b, rows, cols, dt):
        sz = 2 if dt == BF16 else 4
        assert off_b + cols * sz <= AREN, (off_b, cols)
        return arena[0:rows, off_b:off_b + cols * sz].bitcast(dt)

    o = 0

    def nx(cols, dt, rows=128):
        nonlocal o
        ap = vw(o, rows, cols, dt)
        sz = 2 if dt == BF16 else 4
        o += max(32, ((cols * sz + 31) // 32) * 32)
        return ap

    ident_b = nx(128, BF16)
    ident_f = nx(128, F32)
    ones_col = nx(1, BF16)
    ones_row = nx(128, BF16, rows=1)
    c_n30 = nx(1, F32)
    c_eps = nx(1, F32)
    msA = nx(1, F32)
    msB = nx(1, F32)
    rstd = nx(NTT, F32)
    rstd_s = nx(NTT, F32)
    m2row = nx(GT, F32, rows=1)
    varrow = nx(GT, F32, rows=1)
    rstd_row = nx(GT, F32, rows=1)
    nmean2 = nx(NMT, F32)
    rstd2 = nx(NMT, F32)
    msum3 = nx(NMT, F32)
    vsum3 = nx(NMT, F32)
    mpart = nx(NMT, F32)
    vpart = nx(NMT, F32)
    nm3 = nx(NMT, F32)
    rstd3 = nx(NMT, F32)
    tmp1 = nx(1, F32)
    tmp2 = nx(1, F32)
    tmp3 = nx(8, F32)
    b4c = nx(1, F32)
    rstd4 = nx(1, F32)
    nm4 = nx(1, F32)
    rstd2t = nx(1, F32)
    var2t = nx(1, F32)
    rs_cols = nx(H * NTT, F32)
    mpart4 = nx(8, F32)
    nm_row = nx(T, BF16, rows=1)
    colsq = nx(KV, BF16, rows=1)
    colsk = nx(KV, BF16, rows=1)
    colsv = nx(KV, BF16, rows=1)
    gT_all = vw(24 * KB, 128, NMT * 128, BF16)
    Wb = [vw(204 * KB, 128, 512, BF16), vw(205 * KB, 128, 512, BF16)]
    fold_t = vw(204 * KB, 128, 512, F32)
    qtm = [nx(512, BF16), nx(512, BF16)]
    expT = [nx(512, BF16), nx(512, BF16)]
    vfm_st = [nx(512, BF16), nx(512, BF16)]
    se8 = nx(8, F32)
    we8 = nx(8, F32)
    wrep = nx(128, F32)
    gbf = nx(128, BF16)
    nm3rs = nx(1, F32)
    var3 = nx(NMT, F32)
    assert o <= 20 * KB, o
    prod = vw(20 * KB + 2 * KB, 128, 256, F32)

    junk = vw(20 * KB, 128, 4096, F32)
    lnt = vw(20 * KB, 128, 4096, F32)
    junk5 = vw(20 * KB, 128, 512, F32)
    jbf = vw(20 * KB, 128, 512, BF16)
    Bmb = vw(188 * KB, 128, D, BF16)
    wqc = vw(44 * KB, 128, DC * 512, BF16)
    vout = vw(44 * KB, 128, NTT * KV, BF16)
    attn_t = [vw(44 * KB, 128, 4096, F32), vw(60 * KB, 128, 4096, F32)]
    wqkb = vw(44 * KB, 128, DC * 512, BF16)
    ffn_sc = [vw(44 * KB, 128, D, BF16), vw(52 * KB, 128, D, BF16),
              vw(60 * KB, 128, D, BF16), vw(68 * KB, 128, D, BF16),
              vw(164 * KB, 128, D, BF16), vw(156 * KB, 128, D, BF16),
              vw(172 * KB, 128, D, BF16), vw(180 * KB, 128, D, BF16)]
    wkc = vw(76 * KB, 128, DC * 512, BF16)
    wvc = vw(108 * KB, 128, DC * 512, BF16)
    wob = vw(76 * KB, 128, 8 * D, BF16)
    x2cf = vw(76 * KB, 128, DC * MT, BF16)
    x2rd = [vw(76 * KB, 128, 4096, F32), vw(92 * KB, 128, 4096, F32),
            vw(108 * KB, 128, 4096, F32), vw(124 * KB, 128, 4096, F32)]
    xsq = vw(36 * KB, 128, (DC // 2) * GT, BF16)
    xcast = vw(140 * KB, 128, DC * GT, BF16)
    xcast2 = vw(20 * KB, 128, DC * GT, BF16)
    xcg = [xcast, xcast2]
    v_all = vw(140 * KB, 128, NMT * 1024, BF16)
    x2f_st = vw(140 * KB, 128, 4096, F32)
    lnt2 = vw(140 * KB, 128, 4096, F32)
    lnt2b = vw(188 * KB, 128, 4096, F32)
    qfm = vw(172 * KB, 128, H * T, BF16)
    v_allT = vw(156 * KB, 128, 8 * MT, BF16)
    qk_all = vw(140 * KB, 128, NMT * 512, F32)
    h_tmp = [vw(160 * KB, 128, 128, F32), vw(161 * KB, 128, 128, F32)]
    wqraw = [vw(172 * KB, 128, 2048, F32), vw(180 * KB, 128, 2048, F32)]
    kfm = vw(156 * KB, 128, H * T, BF16)
    gbufA = vw(28 * KB, 128, NMT * 512, BF16)
    gbufB = vw(68 * KB, 128, NMT * 512, BF16)
    xmine_sb = vw(172 * KB, 128, 4096, F32)
    vtm = vw(188 * KB, 128, NTT * KV, BF16)
    contrib_st = vw(60 * KB, 128, NMT * 512, BF16)
    x2b_st = vw(188 * KB, 128, 4096, BF16)
    b_bf = vw(28 * KB, 1, D, BF16)
    Ab = vw(196 * KB, 128, DC * 128, BF16)
    Wb4 = [vw(36 * KB + i, 128, 1024, BF16) for i in range(0, 8 * KB, 2 * KB)]

    pbk = [nc.alloc_psum_tensor(f"pb{i}", [128, 2048], U8) for i in range(8)]

    def pv(bank, rows, cols, dt):
        sz = 2 if dt == BF16 else 4
        return pbk[bank][0:rows, 0:cols * sz].bitcast(dt)

    ps_rowf = pv(7, 128, 1, F32)
    ps_st = pv(7, 1, 512, F32)
    ps_cs = pv(6, 1, 512, F32)
    ps_q = [pv(0, 128, 512, F32), pv(1, 128, 512, F32)]
    ps_v = [pv(2, 128, 512, F32), pv(3, 128, 512, F32)]
    ps_k = [pv(4, 128, 256, F32), pv(5, 128, 256, F32)]
    ps_tr = pv(6, 128, 512, BF16)
    ps_s = [pv(0, 128, 512, F32), pv(1, 128, 512, F32)]
    ps_vv = [pv(2, 128, 512, F32), pv(3, 128, 512, F32)]
    ps_sum = [pv(4, 128, 4, F32), pv(5, 128, 4, F32)]
    ps_vt = pv(6, 128, 512, BF16)
    ps_va = [pv(2 + i, 128, 512, BF16) for i in range(4)]
    ps_wo = [pv(6, 128, 512, F32), pv(7, 128, 512, F32),
             pv(0, 128, 512, F32), pv(1, 128, 512, F32)]
    ps_h8 = [pv(i, 128, 128, F32) for i in range(8)]
    ps_g = pv(4, 128, 128, BF16)
    ps_f = [pv(i, 128, 512, F32) for i in range(8)]

    S = Sched()

    def VO(fn, label=None):
        S.op("vector", fn, label=label)

    def SO(fn, label=None):
        S.op("scalar", fn, label=label)

    S.op("gpsimd", lambda g: g.memset(ident_b[:, :], 0.0), inc=("s_gp", "idm"))
    S.wait("gpsimd", "s_gp", "idm")
    S.op("gpsimd", lambda g: g.affine_select(
        out=ident_b[:, :], in_=ident_b[:, :],
        compare_op=ALU.not_equal, fill=1.0, base=0,
        pattern=[[-1, 128]], channel_multiplier=1), inc=("s_gp", "identb"))
    S.op("gpsimd", lambda g: g.memset(ident_f[:, :], 0.0), inc=("s_gp", "idmf"))
    S.wait("gpsimd", "s_gp", "idmf")
    S.op("gpsimd", lambda g: g.affine_select(
        out=ident_f[:, :], in_=ident_f[:, :],
        compare_op=ALU.not_equal, fill=1.0, base=0,
        pattern=[[-1, 128]], channel_multiplier=1), inc=("s_gp", "ident"))
    S.op("gpsimd", lambda g: g.dma_start(out=msA[:, :], in_=msA_d), inc=("s_msk", "msA"), n=16)
    S.op("gpsimd", lambda g: g.dma_start(out=msB[:, :], in_=msB_d), inc=("s_msk", "msB"), n=16)
    def emit_xf(g_i):
        if g_i >= 2:
            S.wait("gpsimd", "s_se", f"kev{g_i - 2}_{H - 1}")
            S.wait("gpsimd", "s_pe", f"v{g_i - 2}_1")
        lo, hi = g_i * GT, (g_i + 1) * GT

        def g_xf(g, lo=lo, hi=hi, g_i=g_i):
            return g.dma_start(
                out=xcg[g_i % 2].rearrange("p (c t) -> p c t", t=GT),
                in_=x_fm[:, lo:hi].rearrange("(c p) t -> p c t", p=128))
        S.op("gpsimd", g_xf, inc=("s_xf" if g_i % 2 == 0 else "s_xg", f"xf{g_i}"), n=16)
    emit_xf(0)
    emit_xf(1)
    S.op("gpsimd", lambda g: g.dma_start(
        out=wkc.rearrange("p (c n) -> p c n", n=512),
        in_=wk_s.rearrange("(c p) n -> p c n", p=128)), inc=("s_wkv", "wk"), n=16)
    S.op("gpsimd", lambda g: g.dma_start(
        out=wvc.rearrange("p (c n) -> p c n", n=512),
        in_=wv_s.rearrange("(c p) n -> p c n", p=128)), inc=("s_wv", "wv"), n=16)
    for g_i in range(2, NG):
        emit_xf(g_i)
    S.wait("gpsimd", "s_pe", f"stat{NG - 1}")
    for idx in range(3):
        S.op("gpsimd", lambda g, og=0, cp=idx, idx=idx: g_w(g, og, cp, idx),
             inc=(f"s_w{idx % 4}", f"W{idx}"), n=16)
    S.wait("gpsimd", "s_se", f"kev{NG - 1}_{H - 1}")
    S.wait("gpsimd", "s_ve", f"vev{NG - 1}_1")
    S.op("gpsimd", lambda g: g.dma_start(
        out=wob.rearrange("p (a n) -> p a n", n=D),
        in_=wo_t.rearrange("(a p) n -> p a n", p=128)), inc=("s_wo", "wo"), n=16)
    HT2 = MT // 2
    S.wait("gpsimd", "s_ct", "contribA_store")
    S.op("gpsimd", lambda g: g.collective_compute(
        "AllGather", ALU.bypass,
        replica_groups=[[0, 1], [2, 3], [4, 5], [6, 7]],
        ins=[contrib.ap()[0:HT2, :]], outs=[gout.ap()[0:MT, :]]),
        inc=("s_cc", "agA"))
    S.wait("gpsimd", "s_ct", "contribB_store")
    S.op("gpsimd", lambda g: g.collective_compute(
        "AllGather", ALU.bypass,
        replica_groups=[[0, 1], [2, 3], [4, 5], [6, 7]],
        ins=[contrib.ap()[HT2:MT, :]], outs=[gout.ap()[MT:2 * MT, :]]),
        inc=("s_cc", "agB"))
    S.wait("gpsimd", "s_ve", f"xm_used{NMT - 1}")
    S.op("gpsimd", lambda g: g.dma_start(
        out=wqkb.rearrange("p (c n) -> p c n", n=512),
        in_=wqk_t.rearrange("(c p) n -> p c n", p=128)), inc=("s_qa", "wqk"), n=16)
    S.op("gpsimd", lambda g: g.dma_start(
        out=Ab.rearrange("p (c n) -> p c n", n=128),
        in_=A_t.rearrange("(c p) n -> p c n", p=128)), inc=("s_qa", "ab"), n=16)
    def g_w(g, og, cp, idx):
        return g.dma_start(
            out=Wb4[idx % 4].rearrange("p (c n) -> p c n", n=512),
            in_=W_t[cp * 256:(cp + 1) * 256,
                    og * 512:(og + 1) * 512].rearrange(
                        "(c p) n -> p c n", p=128))
    for og in range(8):
        for cp in range(DC // 2):
            idx = og * (DC // 2) + cp
            if idx < 3:
                continue
            if idx == 3:
                S.wait("gpsimd", "s_spf", val=16 * NMT)
                S.wait("gpsimd", "s_spb", val=16 * NMT)
                S.op("gpsimd", lambda g: g.dma_start(out=Bmb[:, :], in_=Bm_f),
                     inc=("s_bmb", "bm"), n=16)
            if idx >= 4:
                S.wait("gpsimd", "s_pe", f"Wp{idx - 4}")
            S.op("gpsimd", lambda g, og=og, cp=cp, idx=idx: g_w(g, og, cp, idx),
                 inc=(f"s_w{idx % 4}", f"W{idx}"), n=16)
            if idx == 3:
                S.op("gpsimd", lambda g: g.dma_start(out=b_bf[:, :], in_=b_r),
                     inc=("s_bmb", "b"), n=16)

    def s_wq(s, cc):
        return s.dma_start(out=wqraw[cc % 2][:, :], in_=wq_s[cc * 128:(cc + 1) * 128, :])
    for cc in range(DC):
        if cc >= 2:
            S.wait("sync", "s_ve", f"fold{cc - 2}")
        S.op("sync", lambda s, cc=cc: s_wq(s, cc),
             inc=("s_xa" if cc % 2 == 0 else "s_xb", f"wq{cc}"), n=16)
    HT = MT // 2
    S.wait("sync", "s_ve", "contribA_ready")
    S.op("sync", lambda s: s.dma_start(
        out=contrib.ap()[0:HT, :].rearrange("(j p) n -> p j n", p=128),
        in_=contrib_st[:, 0:4 * 512].rearrange("p (j n) -> p j n", n=512)),
        inc=("s_ct", "contribA_store"), n=16)
    S.wait("sync", "s_cc", val=1)
    S.op("sync", lambda s: s.dma_start(
        out=gbufA[:, 0:4 * 512].rearrange("p (j n) -> p j n", n=512),
        in_=gout.ap()[0:HT, :].rearrange("(j p) n -> p j n", p=128)),
        inc=("s_ga", "gA0"), n=16)
    S.op("sync", lambda s: s.dma_start(
        out=gbufB[:, 0:4 * 512].rearrange("p (j n) -> p j n", n=512),
        in_=gout.ap()[HT:MT, :].rearrange("(j p) n -> p j n", p=128)),
        inc=("s_ga", "gB0"), n=16)
    S.wait("sync", "s_ve", "contrib_ready")
    S.op("sync", lambda s: s.dma_start(
        out=contrib.ap()[HT:MT, :].rearrange("(j p) n -> p j n", p=128),
        in_=contrib_st[:, 4 * 512:8 * 512].rearrange("p (j n) -> p j n", n=512)),
        inc=("s_ct", "contribB_store"), n=16)
    S.wait("sync", "s_cc", val=2)
    S.op("sync", lambda s: s.dma_start(
        out=gbufA[:, 4 * 512:8 * 512].rearrange("p (j n) -> p j n", n=512),
        in_=gout.ap()[MT:MT + HT, :].rearrange("(j p) n -> p j n", p=128)),
        inc=("s_gb", "gA1"), n=16)
    S.op("sync", lambda s: s.dma_start(
        out=gbufB[:, 4 * 512:8 * 512].rearrange("p (j n) -> p j n", n=512),
        in_=gout.ap()[MT + HT:2 * MT, :].rearrange("(j p) n -> p j n", p=128)),
        inc=("s_gb", "gB1"), n=16)
    def s_xm(s, tt):
        return s.dma_start(out=xmine_sb[:, :], in_=x_mine[tt * 128:(tt + 1) * 128, :])
    def s_spf(s, tt):
        return s.dma_start(out=x2f_d.ap()[tt * 128:(tt + 1) * 128, :], in_=x2f_st[:, :])

    def s_spb(s, tt):
        return s.dma_start(
            out=x2b_d.ap().rearrange("(c g p) f -> g p c f", c=DC, p=128)[tt],
            in_=x2b_st.rearrange("p (c f) -> p c f", f=128))
    def s_tr_mk(cc):
        def s_tr(s, cc=cc):
            return s.dma_start_transpose(
                x2cf[:, cc * MT:(cc + 1) * MT],
                x2b_d.ap()[cc * MT:(cc + 1) * MT, :])
        return s_tr
    S.wait("sync", "s_spf", val=16 * NMT)
    S.wait("sync", "s_spb", val=16 * NMT)
    S.wait("sync", "s_ve", f"woev{NMT - 1}_7")
    for cc in range(DC):
        S.op("sync", s_tr_mk(cc), inc=(f"s_t{cc // 8}", f"tr{cc}"), n=16)
    for tt in range(NMT):
        if tt < 4:
            S.wait("sync", "s_pe", f"Wp{7 * (DC // 2) + 4 * tt + 3}")
        else:
            S.wait("sync", f"s_o{tt % 4}", f"st{tt - 4}")

        def s_rd(s, tt=tt):
            return s.dma_start(out=x2rd[tt % 4][:, :],
                               in_=x2f_d.ap()[tt * 128:(tt + 1) * 128, :])
        S.op("sync", s_rd, inc=("s_xa" if tt % 2 == 0 else "s_xb", f"x2rd{tt}"), n=16)
        S.wait("sync", "s_ve", f"out_ready{tt}")

        def s_st(s, tt=tt):
            return s.dma_start(out=out_d[tt * 128:(tt + 1) * 128, :], in_=x2rd[tt % 4][:, :])
        S.op("sync", s_st, inc=(f"s_o{tt % 4}", f"st{tt}"), n=16)
    for i in range(4):
        S.wait("sync", f"s_o{i}", val=16 * (NMT // 4))

    VO(lambda v: v.memset(ones_col[:, :], 1.0))
    VO(lambda v: v.memset(ones_row[:, :], 1.0))
    VO(lambda v: v.memset(c_n30[:, :], EXPB))
    VO(lambda v: v.memset(c_eps[:, :], EPS))
    VO(lambda v: v.memset(msum3[:, :], 0.0))
    VO(lambda v: v.memset(vsum3[:, :], 0.0), label="consts")
    for cc in range(DC):
        S.wait("vector", "s_xa" if cc % 2 == 0 else "s_xb", f"wq{cc}")

        def v_fold_r(v, cc=cc):
            src = wqraw[cc % 2].rearrange("p (k g d) -> p k d g", g=4, d=128)
            return v.tensor_reduce(
                fold_t.rearrange("p (k d) -> p k d", d=128),
                src, axis=AX.X, op=ALU.add)
        VO(v_fold_r)
        VO(lambda v, cc=cc: v.tensor_copy(wqc[:, cc * 512:(cc + 1) * 512], fold_t[:, :]),
           label=f"fold{cc}")
    S.lab[("s_ve", "wqc_done")] = S.tot["s_ve"]
    for g_i in range(NG):
        S.wait("vector", "s_se", f"m2r{g_i}")
        VO(lambda v, g_i=g_i: v.scalar_tensor_tensor(
            varrow[:, :], ps_st[:, GT:2 * GT], INV_D, m2row[:, :],
            ALU.mult, ALU.subtract), label=f"varr{g_i}")
        for t2 in range(2):
            tt = g_i * 2 + t2
            S.wait("vector", "s_se", f"rstdc{tt}")
            S.wait("vector", "s_pe", f"q{g_i}_{t2}")
            if g_i >= 1:
                S.wait("vector", "s_pe", f"qtr{g_i - 1}_{t2}")
            VO(lambda v, tt=tt, t2=t2: v.tensor_scalar(
                qtm[t2][:, :], ps_q[t2][:, :], rstd[:, tt:tt + 1], None, ALU.mult),
               label=f"qev{g_i}_{t2}")
            S.wait("vector", "s_pe", f"v{g_i}_{t2}")
            VO(lambda v, tt=tt, t2=t2: v.tensor_scalar(
                vtm[:, tt * 512:(tt + 1) * 512], ps_v[t2][:, :],
                rstd[:, tt:tt + 1], None, ALU.mult), label=f"vev{g_i}_{t2}")
    for idx in range(H * 4):
        r = idx % 2
        S.wait("vector", "s_pe", f"vs{idx}_15")
        if idx >= 2:
            S.wait("vector", "s_pe", f"vt{idx - 2}")
        VO(lambda v, r=r: v.tensor_copy(vfm_st[r][:, :], ps_vv[r][:, :]),
           label=f"vfm{idx}")
        VO(lambda v, idx=idx, r=r: v.reciprocal(
            rs_cols[:, idx * 4:idx * 4 + 4], ps_sum[r][:, :]), label=f"rs{idx}")
        S.wait("vector", "s_pe", f"vt{idx}")
        qg, h = idx // 4, idx % 4
        for sub in range(4):
            qt = qg * 4 + sub
            VO(lambda v, h=h, qt=qt, sub=sub, idx=idx: v.tensor_scalar(
                vout[:, qt * 512 + h * 128: qt * 512 + (h + 1) * 128],
                ps_vt[:, sub * 128:(sub + 1) * 128],
                rs_cols[:, idx * 4 + sub: idx * 4 + sub + 1], None, ALU.mult),
               label=(f"vout{idx}" if sub == 3 else None))
        if idx == 11:
            S.wait("vector", "s_msk", val=32)
            for j in range(4):
                VO(lambda v, j=j: v.tensor_scalar(
                    jbf[:, :], vout[:, (j + 8) * 512:(j + 9) * 512],
                    msA[:, :], None, ALU.mult))
                VO(lambda v, j=j: v.scalar_tensor_tensor(
                    contrib_st[:, j * 512:(j + 1) * 512],
                    vout[:, j * 512:(j + 1) * 512], msB[:, :],
                    jbf[:, :], ALU.mult, ALU.add),
                   label=("contribA_ready" if j == 3 else None))
    for j in range(4, NMT):
        VO(lambda v, j=j: v.tensor_scalar(
            jbf[:, :], vout[:, (j + 8) * 512:(j + 9) * 512], msA[:, :], None, ALU.mult))
        VO(lambda v, j=j: v.scalar_tensor_tensor(
            contrib_st[:, j * 512:(j + 1) * 512],
            vout[:, j * 512:(j + 1) * 512], msB[:, :],
            jbf[:, :], ALU.mult, ALU.add),
           label=("contrib_ready" if j == NMT - 1 else None))
    S.wait("vector", "s_ga", val=32)
    for j in range(NMT):
        if j == 4:
            S.wait("vector", "s_gb", val=32)
        VO(lambda v, j=j: v.tensor_scalar(
            jbf[:, :], vout[:, j * 512:(j + 1) * 512], msA[:, :], None, ALU.mult))
        VO(lambda v, j=j: v.scalar_tensor_tensor(
            v_all[:, j * 1024:j * 1024 + 512],
            gbufA[:, j * 512:(j + 1) * 512], msB[:, :],
            jbf[:, :], ALU.mult, ALU.add))
        VO(lambda v, j=j: v.tensor_scalar(
            jbf[:, :], vout[:, (j + 8) * 512:(j + 9) * 512], msB[:, :], None, ALU.mult))
        VO(lambda v, j=j: v.scalar_tensor_tensor(
            v_all[:, j * 1024 + 512:(j + 1) * 1024],
            gbufB[:, j * 512:(j + 1) * 512], msA[:, :],
            jbf[:, :], ALU.mult, ALU.add), label=f"va{j}")
    S.lab[("s_ve", "vall_all")] = S.tot["s_ve"]
    for tt in range(NMT):
        for og in range(8):
            S.wait("vector", "s_pe", f"wo{tt}_{og}")
            if og == 0 and tt >= 2:
                S.wait("vector", "s_se", f"ln4_{tt - 2}")
            VO(lambda v, tt=tt, og=og: v.tensor_scalar(
                attn_t[tt % 2][:, og * 512:(og + 1) * 512], ps_wo[og % 4][:, :],
                0.0, 0.0, ALU.add, ALU.add, accum_out=mpart4[:, og:og + 1]),
               label=f"woev{tt}_{og}")
        VO(lambda v: v.reduce_sum(tmp3[:, 6:7], mpart4[:, :], axis=AX.X),
           label=f"m4s_{tt}")
        S.wait("vector", "s_se", f"ln4_{tt}")
        S.wait("vector", "s_xm", f"xm{tt}")
        if tt >= 1:
            S.wait("vector", "s_spf", f"spf{tt - 1}")
        VO(lambda v, tt=tt: v.tensor_scalar(
            lnt[:, :], attn_t[tt % 2][:, :], nm4[:, :], rstd4[:, :], ALU.add, ALU.mult))
        VO(lambda v: v.scalar_tensor_tensor(
            x2f_st[:, :], lnt[:, :], 0.0, xmine_sb[:, :], ALU.add, ALU.add,
            accum_out=tmp3[:, 7:8]), label=f"xm_used{tt}")
        S.wait("vector", "s_se", f"r2_{tt}")
        if tt >= 1:
            S.wait("vector", "s_spb", f"spb{tt - 1}")
        VO(lambda v, tt=tt: v.tensor_scalar(
            x2b_st[:, :], x2f_st[:, :], nmean2[:, tt:tt + 1], rstd2[:, tt:tt + 1],
            ALU.add, ALU.mult), label=f"x2b_ready{tt}")
    for tt in range(NMT):
        S.wait("vector", "s_pe", f"qk{tt}")
        VO(lambda v, tt=tt: v.tensor_copy(qk_all[:, tt * 512:(tt + 1) * 512],
                                          ps_f[tt % 2][:, :]), label=f"qkev{tt}")
        S.wait("vector", "s_pe", f"hh{tt}")
        VO(lambda v, tt=tt: v.tensor_copy(h_tmp[tt % 2][:, :], ps_h8[2 + tt % 2][:, :]),
           label=f"hev{tt}")
        VO(lambda v, tt=tt: v.tensor_mul(prod[:, :], qk_all[:, tt * 512:tt * 512 + 256],
                                         qk_all[:, tt * 512 + 256:(tt + 1) * 512]))
        VO(lambda v: v.tensor_reduce(se8.rearrange("p (e u) -> p e u", u=1),
                                     prod.rearrange("p (e d) -> p e d", d=32),
                                     axis=AX.X, op=ALU.add))
        VO(lambda v: v.reduce_max(tmp1[:, :], se8[:, :], axis=AX.X))
        VO(lambda v: v.tensor_scalar(tmp2[:, :], tmp1[:, :], -ISQ_DK, None, ALU.mult),
           label=f"sm{tt}")
        S.wait("vector", "s_se", f"we{tt}")
        VO(lambda v: v.reduce_sum(tmp1[:, :], we8[:, :], axis=AX.X))
        VO(lambda v: v.reciprocal(tmp2[:, :], tmp1[:, :]))
        VO(lambda v: v.tensor_scalar(we8[:, :], we8[:, :], tmp2[:, :], None, ALU.mult))
        for rr in range(16):
            VO(lambda v, rr=rr: v.tensor_copy(
                wrep.rearrange("p (e r) -> p r e", r=16)[:, rr, :], we8[:, :]))
        if tt >= 1:
            S.wait("vector", "s_pe", f"gt{tt - 1}")
        VO(lambda v, tt=tt: v.tensor_mul(gbf[:, :], h_tmp[tt % 2][:, :], wrep[:, :]),
           label=f"g{tt}")
    S.lab[("s_ve", "p5a_done")] = S.tot["s_ve"]
    for og in range(8):
        S.wait("vector", "s_se", f"fev{og}_{NMT - 1}")
        VO(lambda v: v.tensor_add(msum3[:, :], msum3[:, :], mpart[:, :]))
        VO(lambda v: v.tensor_add(vsum3[:, :], vsum3[:, :], vpart[:, :]),
           label=f"acc{og}")
    for tt in range(NMT):
        VO(lambda v, tt=tt: v.tensor_scalar(nm3[:, tt:tt + 1], msum3[:, tt:tt + 1],
                                            -INV_D, None, ALU.mult))
        VO(lambda v, tt=tt: v.tensor_scalar(tmp1[:, :], msum3[:, tt:tt + 1],
                                            INV_D, None, ALU.mult))
        VO(lambda v: v.tensor_mul(tmp2[:, :], tmp1[:, :], tmp1[:, :]))
        VO(lambda v, tt=tt: v.tensor_scalar(tmp1[:, :], vsum3[:, tt:tt + 1],
                                            INV_D, None, ALU.mult))
        VO(lambda v, tt=tt: v.tensor_sub(var3[:, tt:tt + 1], tmp1[:, :], tmp2[:, :]),
           label=f"f3_{tt}")
    for tt in range(NMT):
        S.wait("vector", "s_se", f"lnt2_{tt}")
        S.wait("vector", "s_xa" if tt % 2 == 0 else "s_xb", f"x2rd{tt}")
        VO(lambda v, tt=tt: v.tensor_add(x2rd[tt % 4][:, :], x2rd[tt % 4][:, :],
                                         (lnt2 if tt % 2 == 0 else lnt2b)[:, :]),
           label=f"out_ready{tt}")

    for g_i in range(NG):
        S.wait("scalar", "s_xf" if g_i % 2 == 0 else "s_xg", f"xf{g_i}")
        if g_i >= 1:
            S.wait("scalar", "s_pe", f"stat{g_i - 1}")
        SO(lambda s, g_i=g_i: s.activation(
            xsq[:, :], xcg[g_i % 2][:, 0:16 * GT], AF.Square), label=f"xsqr{g_i}_0")
        S.wait("scalar", "s_pe", f"sq0_{g_i}")
        SO(lambda s, g_i=g_i: s.activation(
            xsq[:, :], xcg[g_i % 2][:, 16 * GT:32 * GT], AF.Square),
           label=f"xsqr{g_i}_1")
        S.wait("scalar", "s_pe", f"stat{g_i}")
        SO(lambda s, g_i=g_i: s.activation(
            nm_row[:, g_i * GT:(g_i + 1) * GT], ps_st[:, 0:GT], AF.Copy, scale=-INV_D),
           label=f"nmr{g_i}")
        SO(lambda s: s.activation(m2row[:, :], ps_st[:, 0:GT], AF.Square, scale=INV_D),
           label=f"m2r{g_i}")
        if g_i == 0:
            S.wait("scalar", "s_pe", "csk")
            SO(lambda s: s.copy(colsk[:, :], ps_cs[:, :]), label="cske")
        for h in range(H):
            S.wait("scalar", "s_pe", f"k{g_i}_{h}")
            SO(lambda s, g_i=g_i, h=h: s.copy(
                kfm[:, h * T + g_i * GT: h * T + (g_i + 1) * GT], ps_k[h % 2][:, :]),
               label=f"kev{g_i}_{h}")
        S.wait("scalar", "s_ve", f"varr{g_i}")
        SO(lambda s: s.activation(varrow[:, :], varrow[:, :], AF.Ln,
                                  bias=c_eps[0:1, :], scale=1.0))
        SO(lambda s: s.activation(rstd_row[:, :], varrow[:, :], AF.Exp, scale=-0.5),
           label=f"rstdr{g_i}")
        for t2 in range(2):
            tt = g_i * 2 + t2
            S.wait("scalar", "s_pe", f"rtr{tt}")
            SO(lambda s, tt=tt: s.copy(rstd[:, tt:tt + 1], ps_rowf[:, :]),
               label=f"rstdc{tt}")
        SO(lambda s, g_i=g_i: s.activation(
            rstd_s[:, g_i * 2:g_i * 2 + 2], rstd[:, g_i * 2:g_i * 2 + 2],
            AF.Copy, scale=ISQ_HD), label=f"rstds{g_i}")
        if g_i == 0:
            S.wait("scalar", "s_pe", "csv")
            SO(lambda s: s.copy(colsv[:, :], ps_cs[:, :]), label="csve")
            S.wait("scalar", "s_pe", "csq")
            SO(lambda s: s.copy(colsq[:, :], ps_cs[:, :]), label="csqe")
        if g_i == 0:
            S.wait("scalar", "s_ve", "wqc_done")
        for t2 in range(2):
            S.wait("scalar", "s_pe", f"qtr{g_i}_{t2}")
            tt = g_i * 2 + t2
            for h in range(H):
                SO(lambda s, tt=tt, h=h: s.copy(
                    qfm[:, h * T + tt * 128: h * T + (tt + 1) * 128],
                    ps_tr[:, h * 128:(h + 1) * 128]),
                   label=(f"qtrc{g_i}_{t2}" if h == H - 1 else None))
    for idx in range(H * 4):
        for kt in range(NTT):
            S.wait("scalar", "s_pe", f"s{idx}_{kt}")
            if kt >= 2:
                S.wait("scalar", "s_pe", f"vs{idx}_{kt - 2}")
            elif idx > 0:
                S.wait("scalar", "s_pe", f"vs{idx - 1}_{14 + kt}")
            SO(lambda s, kt=kt: s.activation(
                expT[kt % 2][:, :], ps_s[kt % 2][:, :], AF.Exp,
                bias=c_n30[:, :], scale=rstd_s[:, kt:kt + 1]), label=f"e{idx}_{kt}")
    for j in range(NMT):
        for ag in range(2):
            vi = j * 2 + ag
            S.wait("scalar", "s_pe", f"vat{j}_{ag}")
            for k in range(4):
                a = ag * 4 + k
                SO(lambda s, j=j, a=a, k=k, vi=vi: s.copy(
                    v_allT[:, a * MT + j * 128: a * MT + (j + 1) * 128],
                    ps_va[vi % 4][:, k * 128:(k + 1) * 128]),
                   label=(f"vac{j}_{ag}" if k == 3 else None))
    for tt in range(NMT):
        S.wait("scalar", "s_ve", f"m4s_{tt}")
        SO(lambda s, tt=tt: s.activation(junk[:, :], attn_t[tt % 2][:, :], AF.Square,
                                         accum_out=tmp3[:, 2:3]))
        SO(lambda s: s.activation(nm4[:, :], tmp3[:, 6:7], AF.Copy, scale=-INV_D))
        SO(lambda s: s.activation(b4c[:, :], nm4[:, :], AF.Square))
        SO(lambda s: s.activation(b4c[:, :], b4c[:, :], AF.Copy, bias=EPS, scale=-1.0))
        SO(lambda s: s.activation(tmp3[:, 3:4], tmp3[:, 2:3], AF.Ln,
                                  bias=b4c[:, :], scale=INV_D))
        SO(lambda s: s.activation(rstd4[:, :], tmp3[:, 3:4], AF.Exp, scale=-0.5),
           label=f"ln4_{tt}")
        if tt == 0:
            S.wait("scalar", "s_ve", "vall_all")
            S.op("scalar", lambda s: s_xm(s, 0), inc=("s_xm", "xm0"), n=16, dma=True)
        S.wait("scalar", "s_ve", f"xm_used{tt}")
        if tt + 1 < NMT:
            S.op("scalar", lambda s, tt=tt: s_xm(s, tt + 1),
                 inc=("s_xm", f"xm{tt + 1}"), n=16, dma=True)
        S.op("scalar", lambda s, tt=tt: s_spf(s, tt), inc=("s_spf", f"spf{tt}"),
             n=16, dma=True)
        SO(lambda s: s.activation(junk[:, :], x2f_st[:, :], AF.Square,
                                  accum_out=tmp3[:, 4:5]))
        SO(lambda s, tt=tt: s.activation(nmean2[:, tt:tt + 1], tmp3[:, 7:8], AF.Copy,
                                         scale=-INV_D))
        SO(lambda s, tt=tt: s.activation(var2t[:, :], nmean2[:, tt:tt + 1], AF.Square))
        SO(lambda s: s.activation(var2t[:, :], var2t[:, :], AF.Copy, bias=EPS, scale=-1.0))
        SO(lambda s: s.activation(rstd2t[:, :], tmp3[:, 4:5], AF.Ln,
                                  bias=var2t[:, :], scale=INV_D))
        SO(lambda s, tt=tt: s.activation(rstd2[:, tt:tt + 1], rstd2t[:, :], AF.Exp,
                                         scale=-0.5), label=f"r2_{tt}")
        S.wait("scalar", "s_ve", f"x2b_ready{tt}")
        S.op("scalar", lambda s, tt=tt: s_spb(s, tt), inc=("s_spb", f"spb{tt}"),
             n=16, dma=True)
    for tt in range(NMT):
        S.wait("scalar", "s_ve", f"sm{tt}")
        SO(lambda s: s.activation(we8[:, :], se8[:, :], AF.Exp,
                                  bias=tmp2[:, :], scale=ISQ_DK), label=f"we{tt}")
        S.wait("scalar", "s_pe", f"gt{tt}")
        SO(lambda s, tt=tt: s.copy(gT_all[:, tt * 128:(tt + 1) * 128], ps_g[:, :]),
           label=f"gtc{tt}")
    for og in range(8):
        if og >= 1:
            S.wait("scalar", "s_ve", f"acc{og - 1}")
        for tt in range(NMT):
            S.wait("scalar", "s_pe", f"f{og}_{tt}")
            SO(lambda s, og=og, tt=tt: s.activation(
                ffn_sc[tt][:, og * 512:(og + 1) * 512], ps_f[tt][:, :], AF.Copy,
                accum_out=mpart[:, tt:tt + 1]))
            SO(lambda s, tt=tt: s.activation(
                junk5[:, :], ps_f[tt][:, :], AF.Square,
                accum_out=vpart[:, tt:tt + 1]), label=f"fev{og}_{tt}")
    S.wait("scalar", "s_spf", val=16 * NMT)
    for tt in range(NMT):
        S.wait("scalar", "s_ve", f"f3_{tt}")
        SO(lambda s, tt=tt: s.activation(tmp3[:, 5:6], var3[:, tt:tt + 1], AF.Ln,
                                         bias=c_eps[:, :], scale=1.0))
        SO(lambda s, tt=tt: s.activation(rstd3[:, tt:tt + 1], tmp3[:, 5:6], AF.Exp,
                                         scale=-0.5), label=f"r3_{tt}")
        SO(lambda s, tt=tt: s.activation(nm3rs[:, :], nm3[:, tt:tt + 1], AF.Copy,
                                         scale=rstd3[:, tt:tt + 1]))
        if tt >= 2:
            S.wait("scalar", "s_ve", f"out_ready{tt - 2}")
        SO(lambda s, tt=tt: s.activation(
            (lnt2 if tt % 2 == 0 else lnt2b)[:, :], ffn_sc[tt][:, :], AF.Identity,
            bias=nm3rs[:, :], scale=rstd3[:, tt:tt + 1]), label=f"lnt2_{tt}")

    S.wait("tensor", "s_gp", "ident")
    S.wait("tensor", "s_ve", "consts")

    def emit_qtr(g_x):
        for t2 in range(2):
            S.wait("tensor", "s_ve", f"qev{g_x}_{t2}")
            if g_x == 0 and t2 == 0:
                S.wait("tensor", "s_se", "csqe")
            if not (g_x == 0 and t2 == 0):
                pg, pt = (g_x, 0) if t2 == 1 else (g_x - 1, 1)
                S.wait("tensor", "s_se", f"qtrc{pg}_{pt}")

            def t_qtr(t, t2=t2):
                ins = None
                for h in range(H):
                    ins = t.transpose(ps_tr[:, h * 128:(h + 1) * 128],
                                      qtm[t2][:, h * 128:(h + 1) * 128],
                                      ident_b[:, :])
                return ins
            S.op("tensor", t_qtr, inc=("s_pe", f"qtr{g_x}_{t2}"))

    for g_i in range(NG):
        if g_i >= 1:
            emit_qtr(g_i - 1)
        S.wait("tensor", "s_xf" if g_i % 2 == 0 else "s_xg", f"xf{g_i}")
        if g_i >= 1:
            S.wait("tensor", "s_se", f"rstdc{2 * g_i - 1}")

        def t_sum(t, g_i=g_i):
            xc = xcg[g_i % 2]
            ins = None
            for cc in range(DC):
                ins = t.matmul(ps_st[:, 0:GT], ones_col[:, :],
                               xc[:, cc * GT:(cc + 1) * GT],
                               start=(cc == 0), stop=(cc == DC - 1))
            return ins
        S.op("tensor", t_sum, inc=("s_pe", f"sums{g_i}"))
        S.wait("tensor", "s_se", f"xsqr{g_i}_0")

        def t_sq0(t):
            ins = None
            for cc in range(16):
                ins = t.matmul(ps_st[:, GT:2 * GT], ones_col[:, :],
                               xsq[:, cc * GT:(cc + 1) * GT],
                               start=(cc == 0), stop=False)
            return ins
        S.op("tensor", t_sq0, inc=("s_pe", f"sq0_{g_i}"))
        S.wait("tensor", "s_se", f"xsqr{g_i}_1")

        def t_sq1(t):
            ins = None
            for cc in range(16):
                ins = t.matmul(ps_st[:, GT:2 * GT], ones_col[:, :],
                               xsq[:, cc * GT:(cc + 1) * GT],
                               start=False, stop=(cc == 15))
            return ins
        S.op("tensor", t_sq1, inc=("s_pe", f"stat{g_i}"))

        def t_cs_mk(w):
            def t_cs(t, w=w):
                ins = None
                for cc in range(DC):
                    ins = t.matmul(ps_cs[:, :], ones_col[:, :],
                                   w[:, cc * 512:(cc + 1) * 512],
                                   start=(cc == 0), stop=(cc == DC - 1))
                return ins
            return t_cs
        if g_i == 0:
            S.wait("tensor", "s_wkv", val=16)
            S.op("tensor", t_cs_mk(wkc), inc=("s_pe", "csk"))
            S.wait("tensor", "s_se", "cske")
        S.wait("tensor", "s_se", f"nmr{g_i}")
        for h in range(H):
            if g_i >= 1 or h >= 2:
                ppg, pph = (g_i, h - 2) if h >= 2 else (g_i - 1, h + 2)
                S.wait("tensor", "s_se", f"kev{ppg}_{pph}")

            def t_k(t, g_i=g_i, h=h):
                xc = xcg[g_i % 2]
                for cc in range(DC):
                    t.matmul(ps_k[h % 2][:, :],
                             wkc[:, cc * 512 + h * 128: cc * 512 + (h + 1) * 128],
                             xc[:, cc * GT:(cc + 1) * GT],
                             start=(cc == 0), stop=False)
                return t.matmul(ps_k[h % 2][:, :],
                                colsk[:, h * 128:(h + 1) * 128],
                                nm_row[:, g_i * GT:(g_i + 1) * GT],
                                start=False, stop=True)
            S.op("tensor", t_k, inc=("s_pe", f"k{g_i}_{h}"))
        S.wait("tensor", "s_se", f"rstdr{g_i}")
        for t2 in range(2):
            tt = g_i * 2 + t2
            if tt >= 1:
                S.wait("tensor", "s_se", f"rstdc{tt - 1}")

            def t_rtr(t, t2=t2):
                return t.transpose(ps_rowf[:, :],
                                   rstd_row[:, t2 * 128:(t2 + 1) * 128],
                                   ident_f[0:1, 0:1])
            S.op("tensor", t_rtr, inc=("s_pe", f"rtr{tt}"))
        if g_i == 0:
            S.wait("tensor", "s_wv", val=16)
            S.op("tensor", t_cs_mk(wvc), inc=("s_pe", "csv"))
            S.wait("tensor", "s_ve", "wqc_done")
            S.wait("tensor", "s_se", "csve")
            S.op("tensor", t_cs_mk(wqc), inc=("s_pe", "csq"))
            S.wait("tensor", "s_se", "csqe")
        S.wait("tensor", "s_se", f"rstds{g_i}")
        for t2 in range(2):
            tt = g_i * 2 + t2
            if g_i >= 1:
                S.wait("tensor", "s_ve", f"qev{g_i - 1}_{t2}")

            def t_q(t, t2=t2, tt=tt, g_i=g_i):
                xc = xcg[g_i % 2]
                for cc in range(DC):
                    t.matmul(ps_q[t2][:, :],
                             xc[:, cc * GT + t2 * 128: cc * GT + (t2 + 1) * 128],
                             wqc[:, cc * 512:(cc + 1) * 512],
                             start=(cc == 0), stop=False)
                return t.matmul(ps_q[t2][:, :],
                                nm_row[:, tt * 128:(tt + 1) * 128], colsq[:, :],
                                start=False, stop=True)
            S.op("tensor", t_q, inc=("s_pe", f"q{g_i}_{t2}"))
            if g_i >= 1:
                S.wait("tensor", "s_ve", f"vev{g_i - 1}_{t2}")

            def t_v(t, t2=t2, tt=tt, g_i=g_i):
                xc = xcg[g_i % 2]
                for cc in range(DC):
                    t.matmul(ps_v[t2][:, :],
                             xc[:, cc * GT + t2 * 128: cc * GT + (t2 + 1) * 128],
                             wvc[:, cc * 512:(cc + 1) * 512],
                             start=(cc == 0), stop=False)
                return t.matmul(ps_v[t2][:, :],
                                nm_row[:, tt * 128:(tt + 1) * 128], colsv[:, :],
                                start=False, stop=True)
            S.op("tensor", t_v, inc=("s_pe", f"v{g_i}_{t2}"))

    emit_qtr(NG - 1)

    def emit_vtr(idx):
        S.wait("tensor", "s_ve", f"vfm{idx}")
        if idx >= 1:
            S.wait("tensor", "s_ve", f"vout{idx - 1}")

        def t_vtr(t, idx=idx):
            r = idx % 2
            ins = None
            for sub in range(4):
                ins = t.transpose(ps_vt[:, sub * 128:(sub + 1) * 128],
                                  vfm_st[r][:, sub * 128:(sub + 1) * 128],
                                  ident_b[:, :])
            return ins
        S.op("tensor", t_vtr, inc=("s_pe", f"vt{idx}"))

    S.wait("tensor", "s_se", f"qtrc{NG - 1}_1")
    S.wait("tensor", "s_se", f"kev{NG - 1}_{H - 1}")
    S.wait("tensor", "s_ve", f"vev{NG - 1}_1")
    prev_idx = None
    for qg in range(4):
        for h in range(H):
            idx = qg * 4 + h
            r = idx % 2
            def emit_vs(kt, h=h, r=r, idx=idx):
                S.wait("tensor", "s_se", f"e{idx}_{kt}")
                if kt == 0 and idx >= 2:
                    S.wait("tensor", "s_ve", f"vfm{idx - 2}")
                    S.wait("tensor", "s_ve", f"rs{idx - 2}")

                def t_vs(t, h=h, kt=kt, r=r):
                    t.matmul(ps_vv[r][:, :],
                             vtm[:, kt * 512 + h * 128: kt * 512 + (h + 1) * 128],
                             expT[kt % 2][:, :], start=(kt == 0), stop=(kt == NTT - 1))
                    ins = None
                    for sub in range(4):
                        ins = t.matmul(ps_sum[r][:, sub:sub + 1],
                                       expT[kt % 2][:, sub * 128:(sub + 1) * 128],
                                       ones_col[:, :],
                                       start=(kt == 0 and sub == 0),
                                       stop=(kt == NTT - 1 and sub == 3))
                    return ins
                S.op("tensor", t_vs, inc=("s_pe", f"vs{idx}_{kt}"))

            for kt in range(NTT):
                if kt >= 2:
                    S.wait("tensor", "s_se", f"e{idx}_{kt - 2}")
                elif idx >= 1:
                    S.wait("tensor", "s_se", f"e{idx - 1}_{14 + kt}")

                def t_s(t, h=h, qg=qg, kt=kt):
                    return t.matmul(ps_s[kt % 2][:, :],
                                    kfm[:, h * T + kt * 128: h * T + (kt + 1) * 128],
                                    qfm[:, h * T + qg * 512: h * T + (qg + 1) * 512],
                                    start=True, stop=True)
                S.op("tensor", t_s, inc=("s_pe", f"s{idx}_{kt}"))
                if kt >= 1:
                    emit_vs(kt - 1)
            emit_vs(NTT - 1)
            if prev_idx is not None:
                emit_vtr(prev_idx)
            prev_idx = idx
    emit_vtr(prev_idx)
    S.wait("tensor", "s_ve", f"vout{H * 4 - 1}")
    S.wait("tensor", "s_se", f"e{H * 4 - 1}_{NTT - 1}")
    for j in range(NMT):
        S.wait("tensor", "s_ve", f"va{j}")
        for ag in range(2):
            vi = j * 2 + ag
            if vi >= 4:
                p = vi - 4
                S.wait("tensor", "s_se", f"vac{p // 2}_{p % 2}")

            def t_vat(t, j=j, ag=ag, vi=vi):
                ins = None
                for k in range(4):
                    a = ag * 4 + k
                    ins = t.transpose(ps_va[vi % 4][:, k * 128:(k + 1) * 128],
                                      v_all[:, j * 1024 + a * 128: j * 1024 + (a + 1) * 128],
                                      ident_b[:, :])
                return ins
            S.op("tensor", t_vat, inc=("s_pe", f"vat{j}_{ag}"))
    S.wait("tensor", "s_wo", val=16)
    S.wait("tensor", "s_se", f"vac{NMT - 1}_1")
    S.wait("tensor", "s_se", f"e{H * 4 - 1}_{NTT - 1}")
    for tt in range(NMT):
        for og in range(8):
            if tt * 8 + og >= 4:
                p = tt * 8 + og - 4
                S.wait("tensor", "s_ve", f"woev{p // 8}_{p % 8}")

            def t_wo(t, tt=tt, og=og):
                ins = None
                for a in range(8):
                    ins = t.matmul(ps_wo[og % 4][:, :],
                                   v_allT[:, a * MT + tt * 128: a * MT + (tt + 1) * 128],
                                   wob[:, a * D + og * 512: a * D + (og + 1) * 512],
                                   start=(a == 0), stop=(a == 7))
                return ins
            S.op("tensor", t_wo, inc=("s_pe", f"wo{tt}_{og}"))
    def emit_gt(tt):
        S.wait("tensor", "s_ve", f"g{tt}")
        if tt >= 1:
            S.wait("tensor", "s_se", f"gtc{tt - 1}")

        def t_gt(t):
            return t.transpose(ps_g[:, :], gbf[:, :], ident_b[:, :])
        S.op("tensor", t_gt, inc=("s_pe", f"gt{tt}"))

    S.wait("tensor", "s_qa", val=32)
    for i in range(4):
        S.wait("tensor", f"s_t{i}", val=16 * 8)
    for tt in range(NMT):
        if tt >= 2:
            S.wait("tensor", "s_ve", f"qkev{tt - 2}")

        def t_qk(t, tt=tt):
            ins = None
            for cc in range(DC):
                ins = t.matmul(ps_f[tt % 2][:, :],
                               x2cf[:, cc * MT + tt * 128: cc * MT + (tt + 1) * 128],
                               wqkb[:, cc * 512:(cc + 1) * 512],
                               start=(cc == 0), stop=(cc == DC - 1))
            return ins
        S.op("tensor", t_qk, inc=("s_pe", f"qk{tt}"))
        if tt >= 2:
            S.wait("tensor", "s_ve", f"hev{tt - 2}")

        def t_h(t, tt=tt):
            ins = None
            for cc in range(DC):
                ins = t.matmul(ps_h8[2 + tt % 2][:, :],
                               x2cf[:, cc * MT + tt * 128: cc * MT + (tt + 1) * 128],
                               Ab[:, cc * 128:(cc + 1) * 128],
                               start=(cc == 0), stop=(cc == DC - 1))
            return ins
        S.op("tensor", t_h, inc=("s_pe", f"hh{tt}"))
        if tt >= 1:
            emit_gt(tt - 1)
    emit_gt(NMT - 1)
    S.wait("tensor", "s_bmb", val=32)
    S.wait("tensor", "s_ve", "p5a_done")
    S.wait("tensor", "s_se", f"gtc{NMT - 1}")
    for og in range(8):
        for cp in range(DC // 2):
            idx = og * (DC // 2) + cp
            S.wait("tensor", f"s_w{idx % 4}", f"W{idx}")
            if cp == 0 and og >= 1:
                for tt in range(NMT):
                    S.wait("tensor", "s_se", f"fev{og - 1}_{tt}")

                    def t_w0(t, tt=tt, idx=idx):
                        ins = None
                        for cc in range(2):
                            ins = t.matmul(
                                ps_f[tt][:, :],
                                x2cf[:, cc * MT + tt * 128: cc * MT + (tt + 1) * 128],
                                Wb4[idx % 4][:, cc * 512:(cc + 1) * 512],
                                start=(cc == 0), stop=False)
                        return ins
                    if tt < NMT - 1:
                        S.op("tensor", t_w0)
                    else:
                        S.op("tensor", t_w0, inc=("s_pe", f"Wp{idx}"))
            elif cp < DC // 2 - 1:
                def t_w(t, cp=cp, idx=idx):
                    ins = None
                    for half in range(2):
                        cc = cp * 2 + half
                        for tt in range(NMT):
                            ins = t.matmul(
                                ps_f[tt][:, :],
                                x2cf[:, cc * MT + tt * 128: cc * MT + (tt + 1) * 128],
                                Wb4[idx % 4][:, half * 512:(half + 1) * 512],
                                start=(cc == 0), stop=False)
                    return ins
                S.op("tensor", t_w, inc=("s_pe", f"Wp{idx}"))
            else:
                def t_w30(t, cp=cp, idx=idx):
                    cc = cp * 2
                    ins = None
                    for tt in range(NMT):
                        ins = t.matmul(
                            ps_f[tt][:, :],
                            x2cf[:, cc * MT + tt * 128: cc * MT + (tt + 1) * 128],
                            Wb4[idx % 4][:, 0:512], start=False, stop=False)
                    return ins
                S.op("tensor", t_w30, inc=("s_pe", f"Wc30_{og}"))
                for tt in range(NMT):
                    def t_wt(t, cp=cp, idx=idx, og=og, tt=tt):
                        cc = cp * 2 + 1
                        t.matmul(ps_f[tt][:, :],
                                 x2cf[:, cc * MT + tt * 128: cc * MT + (tt + 1) * 128],
                                 Wb4[idx % 4][:, 512:1024], start=False, stop=False)
                        t.matmul(ps_f[tt][:, :], ones_row[:, :],
                                 b_bf[:, og * 512:(og + 1) * 512],
                                 start=False, stop=False)
                        return t.matmul(ps_f[tt][:, :], gT_all[:, tt * 128:(tt + 1) * 128],
                                        Bmb[:, og * 512:(og + 1) * 512],
                                        start=False, stop=True)
                    if tt < NMT - 1:
                        S.op("tensor", t_wt, inc=("s_pe", f"f{og}_{tt}"))
                    else:
                        S.op("tensor", t_wt, inc=("s_pe", f"Wp{idx}"))
                        S.lab[("s_pe", f"f{og}_{NMT - 1}")] = S.tot["s_pe"]

    sem_names = ["s_xa", "s_xb", "s_xm", "s_ct", "s_ga", "s_msk", "s_wkv", "s_wv", "s_gb",
                 "s_bmb", "s_wo", "s_qa", "s_xf",
                 "s_spf", "s_spb", "s_oa", "s_ob", "s_pe", "s_ve", "s_se",
                 "s_gp", "s_cc", "s_tr", "s_xg"] + [f"s_w{i}" for i in range(8)]
    sem_names += [f"s_t{i}" for i in range(4)]
    sem_names = [s for s in sem_names if s not in ("s_oa", "s_ob")] + [f"s_o{i}" for i in range(4)]
    import contextlib
    with contextlib.ExitStack() as stk:
        sems = {n: stk.enter_context(nc.semaphore(n)) for n in sem_names}
        block = stk.enter_context(nc.Block())
        block.sync(lambda e: S.emit("sync", e, sems))
        block.gpsimd(lambda e: S.emit("gpsimd", e, sems))
        block.tensor(lambda e: S.emit("tensor", e, sems))
        block.vector(lambda e: S.emit("vector", e, sems))
        block.scalar(lambda e: S.emit("scalar", e, sems))
    return nc


_NC_CACHE = None


def kernel(L0_attn_x, L0_attn_wq, L0_attn_wk, L0_attn_wv, L0_attn_wo,
           L0_ffn_W, L0_ffn_b, L0_ffn_Wq, L0_ffn_Wk, L0_ffn_A, L0_ffn_Bm):
    global _NC_CACHE
    x = np.asarray(L0_attn_x, dtype=np.float32)
    wq = np.asarray(L0_attn_wq, np.float32)
    wk = np.asarray(L0_attn_wk, np.float32)
    wv = np.asarray(L0_attn_wv, np.float32)
    wo = np.asarray(L0_attn_wo, np.float32)
    W = np.asarray(L0_ffn_W, np.float32)
    b = np.asarray(L0_ffn_b, np.float32)
    Wq = np.asarray(L0_ffn_Wq, np.float32)
    Wk = np.asarray(L0_ffn_Wk, np.float32)
    A = np.asarray(L0_ffn_A, np.float32)
    Bm = np.asarray(L0_ffn_Bm, np.float32)

    W_t = np.ascontiguousarray(W.T)
    wqk_t = np.ascontiguousarray(np.concatenate([Wq, Wk], axis=0).T)
    A_t = np.ascontiguousarray(A.reshape(128, D).T)
    Bm_f = np.ascontiguousarray(Bm.transpose(0, 2, 1).reshape(128, D))
    b_rs = np.ascontiguousarray(b.reshape(1, D))
    wo_c = np.ascontiguousarray(wo)

    in_maps = []
    for c in range(8):
        bi, hi = c // 2, c % 2
        in_maps.append({
            "x_tm": np.ascontiguousarray(x[bi]),
            "x_fm": np.ascontiguousarray(x[bi].T),
            "x_mine": np.ascontiguousarray(x[bi, hi * MT:(hi + 1) * MT]),
            "wq_s": np.ascontiguousarray(wq[:, hi * 2048:(hi + 1) * 2048]),
            "wk_s": np.ascontiguousarray(wk[:, hi * KV:(hi + 1) * KV]),
            "wv_s": np.ascontiguousarray(wv[:, hi * KV:(hi + 1) * KV]),
            "wo_t": wo_c,
            "W_t": W_t,
            "b_r": b_rs,
            "wqk_t": wqk_t,
            "A_t": A_t,
            "Bm_f": Bm_f,
            "msA_d": np.full((128, 1), 1.0 - hi, np.float32),
            "msB_d": np.full((128, 1), float(hi), np.float32),
        })
    if _NC_CACHE is None:
        _NC_CACHE = build_nc()
    res = run_bass_kernel_spmd(_NC_CACHE, in_maps, core_ids=list(range(8)))
    out = np.empty((4, T, D), np.float32)
    for c in range(8):
        bi, hi = c // 2, c % 2
        out[bi, hi * MT:(hi + 1) * MT] = res.results[c]["out"]
    return out


# version 18
# speedup vs baseline: 1.0481x; 1.0476x over previous
import numpy as np
import ml_dtypes

import concourse.bass as bass
import concourse.mybir as mybir
from concourse.bass_utils import run_bass_kernel_spmd

F32 = mybir.dt.float32
BF16 = mybir.dt.bfloat16
U8 = mybir.dt.uint8
AF = mybir.ActivationFunctionType
ALU = mybir.AluOpType
AX = mybir.AxisListType

T = 2048
D = 4096
MT = 1024
NTT = 16
NMT = 8
DC = 32
H = 4
KV = 512
NG = 8
GT = 256
ISQ_HD = float(1.0 / np.sqrt(128.0))
ISQ_DK = float(1.0 / np.sqrt(32.0))
INV_D = float(1.0 / 4096.0)
EPS = 1e-5
EXPB = -30.0
KB = 1024

CHAIN_SEM = {"vector": "s_ve", "scalar": "s_se"}


class Sched:
    ENGINES = ("sync", "gpsimd", "tensor", "vector", "scalar")

    def __init__(self):
        self.steps = {k: [] for k in self.ENGINES}
        self.tot = {}
        self.lab = {}

    def op(self, eng, fn, label=None, inc=None, n=1, dma=False):
        if eng in CHAIN_SEM and not dma:
            assert inc is None
            sem = CHAIN_SEM[eng]
            t = self.tot.get(sem, 0) + 1
            self.tot[sem] = t
            if label is not None:
                assert (sem, label) not in self.lab
                self.lab[(sem, label)] = t
            self.steps[eng].append(("cop", fn, sem, t))
        else:
            if inc is not None:
                sem, lbl = inc
                t = self.tot.get(sem, 0) + n
                self.tot[sem] = t
                assert (sem, lbl) not in self.lab, (sem, lbl)
                self.lab[(sem, lbl)] = t
            self.steps[eng].append(("op", fn, inc, n))

    def wait(self, eng, sem, label=None, val=None):
        assert (label is None) != (val is None)
        self.steps[eng].append(("wait", sem, label, val))

    def emit(self, eng_name, eng, sems):
        cnt = 0
        for step in self.steps[eng_name]:
            if step[0] == "wait":
                _, sem, label, val = step
                v = val if val is not None else self.lab[(sem, label)]
                eng.wait_ge(sems[sem], v)
            elif step[0] == "cop":
                _, fn, sem, t = step
                if cnt > 0:
                    eng.wait_ge(sems[sem], cnt)
                instr = fn(eng)
                assert instr is not None
                instr.then_inc(sems[sem], 1)
                cnt = t
            else:
                _, fn, inc, n = step
                instr = fn(eng)
                if inc is not None:
                    assert instr is not None
                    instr.then_inc(sems[inc[0]], n)


def build_nc():
    nc = bass.Bass("TRN2", target_bir_lowering=False)

    x_fm = nc.dram_tensor("x_fm", [D, T], F32, kind="ExternalInput").ap()
    x_mine = nc.dram_tensor("x_mine", [MT, D], F32, kind="ExternalInput").ap()
    wq_s = nc.dram_tensor("wq_s", [D, 2048], F32, kind="ExternalInput").ap()
    wk_s = nc.dram_tensor("wk_s", [D, KV], F32, kind="ExternalInput").ap()
    wv_s = nc.dram_tensor("wv_s", [D, KV], F32, kind="ExternalInput").ap()
    wo_t = nc.dram_tensor("wo_t", [1024, D], F32, kind="ExternalInput").ap()
    W_t = nc.dram_tensor("W_t", [D, D], F32, kind="ExternalInput").ap()
    b_r = nc.dram_tensor("b_r", [1, D], F32, kind="ExternalInput").ap()
    wqk_t = nc.dram_tensor("wqk_t", [D, 512], F32, kind="ExternalInput").ap()
    A_t = nc.dram_tensor("A_t", [D, 128], F32, kind="ExternalInput").ap()
    Bm_f = nc.dram_tensor("Bm_f", [128, D], F32, kind="ExternalInput").ap()
    msA_d = nc.dram_tensor("msA_d", [128, 1], F32, kind="ExternalInput").ap()
    msB_d = nc.dram_tensor("msB_d", [128, 1], F32, kind="ExternalInput").ap()
    out_d = nc.dram_tensor("out", [MT, D], F32, kind="ExternalOutput").ap()

    contrib = nc.dram_tensor("contrib", [MT, KV], BF16, kind="Internal")
    gout = nc.dram_tensor("gout", [2 * MT, KV], BF16, kind="Internal")
    x2f_d = nc.dram_tensor("x2f_d", [MT, D], F32, kind="Internal")
    x2b_d = nc.dram_tensor("x2b_d", [DC * MT, 128], BF16, kind="Internal")

    AREN = 207 * KB
    arena = nc.alloc_sbuf_tensor("arena", [128, AREN], U8)

    def vw(off_b, rows, cols, dt):
        sz = 2 if dt == BF16 else 4
        assert off_b + cols * sz <= AREN, (off_b, cols)
        return arena[0:rows, off_b:off_b + cols * sz].bitcast(dt)

    o = 0

    def nx(cols, dt, rows=128):
        nonlocal o
        ap = vw(o, rows, cols, dt)
        sz = 2 if dt == BF16 else 4
        o += max(32, ((cols * sz + 31) // 32) * 32)
        return ap

    ident_b = nx(128, BF16)
    ident_f = nx(128, F32)
    ones_col = nx(1, BF16)
    ones_row = nx(128, BF16, rows=1)
    c_n30 = nx(1, F32)
    c_eps = nx(1, F32)
    msA = nx(1, F32)
    msB = nx(1, F32)
    rstd = nx(NTT, F32)
    rstd_s = nx(NTT, F32)
    m2row = nx(GT, F32, rows=1)
    varrow = nx(GT, F32, rows=1)
    rstd_row = nx(GT, F32, rows=1)
    nmean2 = nx(NMT, F32)
    rstd2 = nx(NMT, F32)
    msum3 = nx(NMT, F32)
    vsum3 = nx(NMT, F32)
    mpart = nx(NMT, F32)
    vpart = nx(NMT, F32)
    nm3 = nx(NMT, F32)
    rstd3 = nx(NMT, F32)
    tmp1 = nx(1, F32)
    tmp2 = nx(1, F32)
    tmp3 = nx(8, F32)
    b4c = nx(1, F32)
    rstd4 = nx(1, F32)
    nm4 = nx(1, F32)
    rstd2t = nx(1, F32)
    var2t = nx(1, F32)
    rs_cols = nx(H * NTT, F32)
    mpart4 = nx(8, F32)
    nm_row = nx(T, BF16, rows=1)
    colsq = nx(KV, BF16, rows=1)
    colsk = nx(KV, BF16, rows=1)
    colsv = nx(KV, BF16, rows=1)
    gT_all = vw(24 * KB, 128, NMT * 128, BF16)
    Wb = [vw(204 * KB, 128, 512, BF16), vw(205 * KB, 128, 512, BF16)]
    fold_t = vw(204 * KB, 128, 512, F32)
    qtm = [nx(512, BF16), nx(512, BF16)]
    expT = [nx(512, BF16), nx(512, BF16)]
    vfm_st = [nx(512, BF16), nx(512, BF16)]
    se8 = nx(8, F32)
    we8 = nx(8, F32)
    wrep = nx(128, F32)
    gbf = nx(128, BF16)
    nm3rs = nx(1, F32)
    var3 = nx(NMT, F32)
    assert o <= 20 * KB, o
    prod = vw(20 * KB + 2 * KB, 128, 256, F32)

    junk = vw(20 * KB, 128, 4096, F32)
    lnt = vw(20 * KB, 128, 4096, F32)
    junk5 = vw(20 * KB, 128, 512, F32)
    jbf = vw(20 * KB, 128, 512, BF16)
    Bmb = vw(188 * KB, 128, D, BF16)
    wqc = vw(44 * KB, 128, DC * 512, BF16)
    vout = vw(44 * KB, 128, NTT * KV, BF16)
    attn_t = [vw(44 * KB, 128, 4096, F32), vw(60 * KB, 128, 4096, F32)]
    wqkb = vw(44 * KB, 128, DC * 512, BF16)
    ffn_sc = [vw(44 * KB, 128, D, BF16), vw(52 * KB, 128, D, BF16),
              vw(60 * KB, 128, D, BF16), vw(68 * KB, 128, D, BF16),
              vw(164 * KB, 128, D, BF16), vw(156 * KB, 128, D, BF16),
              vw(172 * KB, 128, D, BF16), vw(180 * KB, 128, D, BF16)]
    wkc = vw(76 * KB, 128, DC * 512, BF16)
    wvc = vw(108 * KB, 128, DC * 512, BF16)
    wob = vw(76 * KB, 128, 8 * D, BF16)
    x2cf = vw(76 * KB, 128, DC * MT, BF16)
    x2rd = [vw(76 * KB, 128, 4096, F32), vw(92 * KB, 128, 4096, F32),
            vw(108 * KB, 128, 4096, F32), vw(124 * KB, 128, 4096, F32)]
    xsq = vw(36 * KB, 128, (DC // 2) * GT, BF16)
    xcast = vw(140 * KB, 128, DC * GT, BF16)
    xcast2 = vw(20 * KB, 128, DC * GT, BF16)
    xcg = [xcast, xcast2]
    v_all = vw(140 * KB, 128, NMT * 1024, BF16)
    x2f_st = vw(140 * KB, 128, 4096, F32)
    lnt2 = vw(140 * KB, 128, 4096, F32)
    lnt2b = vw(188 * KB, 128, 4096, F32)
    qfm = vw(172 * KB, 128, H * T, BF16)
    v_allT = vw(156 * KB, 128, 8 * MT, BF16)
    qk_all = vw(140 * KB, 128, NMT * 512, F32)
    h_tmp = [vw(160 * KB, 128, 128, F32), vw(161 * KB, 128, 128, F32)]
    wqraw = [vw(172 * KB, 128, 2048, F32), vw(180 * KB, 128, 2048, F32)]
    kfm = vw(156 * KB, 128, H * T, BF16)
    gbufA = vw(28 * KB, 128, NMT * 512, BF16)
    gbufB = vw(68 * KB, 128, NMT * 512, BF16)
    xmine_sb = vw(172 * KB, 128, 4096, F32)
    vtm = vw(188 * KB, 128, NTT * KV, BF16)
    contrib_st = vw(60 * KB, 128, NMT * 512, BF16)
    x2b_st = vw(188 * KB, 128, 4096, BF16)
    b_bf = vw(28 * KB, 1, D, BF16)
    Ab = vw(196 * KB, 128, DC * 128, BF16)
    Wb4 = [vw(36 * KB + i, 128, 1024, BF16) for i in range(0, 8 * KB, 2 * KB)]

    pbk = [nc.alloc_psum_tensor(f"pb{i}", [128, 2048], U8) for i in range(8)]

    def pv(bank, rows, cols, dt):
        sz = 2 if dt == BF16 else 4
        return pbk[bank][0:rows, 0:cols * sz].bitcast(dt)

    ps_rowf = pv(7, 128, 1, F32)
    ps_st = pv(7, 1, 512, F32)
    ps_cs = pv(6, 1, 512, F32)
    ps_q = [pv(0, 128, 512, F32), pv(1, 128, 512, F32)]
    ps_v = [pv(2, 128, 512, F32), pv(3, 128, 512, F32)]
    ps_k = [pv(4, 128, 256, F32), pv(5, 128, 256, F32)]
    ps_tr = pv(6, 128, 512, BF16)
    ps_s = [pv(0, 128, 512, F32), pv(1, 128, 512, F32)]
    ps_vv = [pv(2, 128, 512, F32), pv(3, 128, 512, F32)]
    ps_sum = [pv(4, 128, 4, F32), pv(5, 128, 4, F32)]
    ps_vt = pv(6, 128, 512, BF16)
    ps_va = [pv(2 + i, 128, 512, BF16) for i in range(4)]
    ps_wo = [pv(6, 128, 512, F32), pv(7, 128, 512, F32),
             pv(0, 128, 512, F32), pv(1, 128, 512, F32)]
    ps_h8 = [pv(i, 128, 128, F32) for i in range(8)]
    ps_g = pv(4, 128, 128, BF16)
    ps_f = [pv(i, 128, 512, F32) for i in range(8)]

    S = Sched()

    def VO(fn, label=None):
        S.op("vector", fn, label=label)

    def SO(fn, label=None):
        S.op("scalar", fn, label=label)

    S.op("gpsimd", lambda g: g.memset(ident_b[:, :], 0.0), inc=("s_gp", "idm"))
    S.wait("gpsimd", "s_gp", "idm")
    S.op("gpsimd", lambda g: g.affine_select(
        out=ident_b[:, :], in_=ident_b[:, :],
        compare_op=ALU.not_equal, fill=1.0, base=0,
        pattern=[[-1, 128]], channel_multiplier=1), inc=("s_gp", "identb"))
    S.op("gpsimd", lambda g: g.memset(ident_f[:, :], 0.0), inc=("s_gp", "idmf"))
    S.wait("gpsimd", "s_gp", "idmf")
    S.op("gpsimd", lambda g: g.affine_select(
        out=ident_f[:, :], in_=ident_f[:, :],
        compare_op=ALU.not_equal, fill=1.0, base=0,
        pattern=[[-1, 128]], channel_multiplier=1), inc=("s_gp", "ident"))
    S.op("gpsimd", lambda g: g.dma_start(out=msA[:, :], in_=msA_d), inc=("s_msk", "msA"), n=16)
    S.op("gpsimd", lambda g: g.dma_start(out=msB[:, :], in_=msB_d), inc=("s_msk", "msB"), n=16)
    def emit_xf(g_i):
        if g_i >= 2:
            S.wait("gpsimd", "s_se", f"kev{g_i - 2}_{H - 1}")
            S.wait("gpsimd", "s_pe", f"v{g_i - 2}_1")
        lo, hi = g_i * GT, (g_i + 1) * GT

        def g_xf(g, lo=lo, hi=hi, g_i=g_i):
            return g.dma_start(
                out=xcg[g_i % 2].rearrange("p (c t) -> p c t", t=GT),
                in_=x_fm[:, lo:hi].rearrange("(c p) t -> p c t", p=128))
        S.op("gpsimd", g_xf, inc=("s_xf" if g_i % 2 == 0 else "s_xg", f"xf{g_i}"), n=16)
    emit_xf(0)
    emit_xf(1)
    S.op("gpsimd", lambda g: g.dma_start(
        out=wkc.rearrange("p (c n) -> p c n", n=512),
        in_=wk_s.rearrange("(c p) n -> p c n", p=128)), inc=("s_wkv", "wk"), n=16)
    S.op("gpsimd", lambda g: g.dma_start(
        out=wvc.rearrange("p (c n) -> p c n", n=512),
        in_=wv_s.rearrange("(c p) n -> p c n", p=128)), inc=("s_wv", "wv"), n=16)
    for g_i in range(2, NG):
        emit_xf(g_i)
    S.wait("gpsimd", "s_pe", f"stat{NG - 1}")
    for idx in range(3):
        S.op("gpsimd", lambda g, og=0, cp=idx, idx=idx: g_w(g, og, cp, idx),
             inc=(f"s_w{idx % 4}", f"W{idx}"), n=16)
    S.wait("gpsimd", "s_se", f"kev{NG - 1}_{H - 1}")
    S.wait("gpsimd", "s_ve", f"vev{NG - 1}_1")
    S.op("gpsimd", lambda g: g.dma_start(
        out=wob.rearrange("p (a n) -> p a n", n=D),
        in_=wo_t.rearrange("(a p) n -> p a n", p=128)), inc=("s_wo", "wo"), n=16)
    HT2 = MT // 2
    S.wait("gpsimd", "s_ct", "contribA_store")
    S.op("gpsimd", lambda g: g.collective_compute(
        "AllGather", ALU.bypass,
        replica_groups=[[0, 1], [2, 3], [4, 5], [6, 7]],
        ins=[contrib.ap()[0:HT2, :]], outs=[gout.ap()[0:MT, :]]),
        inc=("s_cc", "agA"))
    S.wait("gpsimd", "s_ct", "contribB_store")
    S.op("gpsimd", lambda g: g.collective_compute(
        "AllGather", ALU.bypass,
        replica_groups=[[0, 1], [2, 3], [4, 5], [6, 7]],
        ins=[contrib.ap()[HT2:MT, :]], outs=[gout.ap()[MT:2 * MT, :]]),
        inc=("s_cc", "agB"))
    S.wait("gpsimd", "s_ve", f"xm_used{NMT - 1}")
    S.op("gpsimd", lambda g: g.dma_start(
        out=wqkb.rearrange("p (c n) -> p c n", n=512),
        in_=wqk_t.rearrange("(c p) n -> p c n", p=128)), inc=("s_qa", "wqk"), n=16)
    S.op("gpsimd", lambda g: g.dma_start(
        out=Ab.rearrange("p (c n) -> p c n", n=128),
        in_=A_t.rearrange("(c p) n -> p c n", p=128)), inc=("s_qa", "ab"), n=16)
    def g_w(g, og, cp, idx):
        return g.dma_start(
            out=Wb4[idx % 4].rearrange("p (c n) -> p c n", n=512),
            in_=W_t[cp * 256:(cp + 1) * 256,
                    og * 512:(og + 1) * 512].rearrange(
                        "(c p) n -> p c n", p=128))
    for og in range(8):
        for cp in range(DC // 2):
            idx = og * (DC // 2) + cp
            if idx < 3:
                continue
            if idx == 3:
                S.wait("gpsimd", "s_spf", val=16 * NMT)
                S.wait("gpsimd", "s_spb", val=16 * NMT)
                S.op("gpsimd", lambda g: g.dma_start(out=Bmb[:, :], in_=Bm_f),
                     inc=("s_bmb", "bm"), n=16)
            if idx >= 4:
                S.wait("gpsimd", "s_pe", f"Wp{idx - 4}")
            S.op("gpsimd", lambda g, og=og, cp=cp, idx=idx: g_w(g, og, cp, idx),
                 inc=(f"s_w{idx % 4}", f"W{idx}"), n=16)
            if idx == 3:
                S.op("gpsimd", lambda g: g.dma_start(out=b_bf[:, :], in_=b_r),
                     inc=("s_bmb", "b"), n=16)

    def s_wq(s, cc):
        return s.dma_start(out=wqraw[cc % 2][:, :], in_=wq_s[cc * 128:(cc + 1) * 128, :])
    for cc in range(DC):
        if cc >= 2:
            S.wait("sync", "s_ve", f"fold{cc - 2}")
        S.op("sync", lambda s, cc=cc: s_wq(s, cc),
             inc=("s_xa" if cc % 2 == 0 else "s_xb", f"wq{cc}"), n=16)
    HT = MT // 2
    S.wait("sync", "s_ve", "contribA_ready")
    S.op("sync", lambda s: s.dma_start(
        out=contrib.ap()[0:HT, :].rearrange("(j p) n -> p j n", p=128),
        in_=contrib_st[:, 0:4 * 512].rearrange("p (j n) -> p j n", n=512)),
        inc=("s_ct", "contribA_store"), n=16)
    S.wait("sync", "s_cc", val=1)
    S.op("sync", lambda s: s.dma_start(
        out=gbufA[:, 0:4 * 512].rearrange("p (j n) -> p j n", n=512),
        in_=gout.ap()[0:HT, :].rearrange("(j p) n -> p j n", p=128)),
        inc=("s_ga", "gA0"), n=16)
    S.op("sync", lambda s: s.dma_start(
        out=gbufB[:, 0:4 * 512].rearrange("p (j n) -> p j n", n=512),
        in_=gout.ap()[HT:MT, :].rearrange("(j p) n -> p j n", p=128)),
        inc=("s_ga", "gB0"), n=16)
    S.wait("sync", "s_ve", "contrib_ready")
    S.op("sync", lambda s: s.dma_start(
        out=contrib.ap()[HT:MT, :].rearrange("(j p) n -> p j n", p=128),
        in_=contrib_st[:, 4 * 512:8 * 512].rearrange("p (j n) -> p j n", n=512)),
        inc=("s_ct", "contribB_store"), n=16)
    S.wait("sync", "s_cc", val=2)
    S.op("sync", lambda s: s.dma_start(
        out=gbufA[:, 4 * 512:8 * 512].rearrange("p (j n) -> p j n", n=512),
        in_=gout.ap()[MT:MT + HT, :].rearrange("(j p) n -> p j n", p=128)),
        inc=("s_gb", "gA1"), n=16)
    S.op("sync", lambda s: s.dma_start(
        out=gbufB[:, 4 * 512:8 * 512].rearrange("p (j n) -> p j n", n=512),
        in_=gout.ap()[MT + HT:2 * MT, :].rearrange("(j p) n -> p j n", p=128)),
        inc=("s_gb", "gB1"), n=16)
    def s_xm(s, tt):
        return s.dma_start(out=xmine_sb[:, :], in_=x_mine[tt * 128:(tt + 1) * 128, :])
    def s_spf(s, tt):
        return s.dma_start(out=x2f_d.ap()[tt * 128:(tt + 1) * 128, :], in_=x2f_st[:, :])

    def s_spb(s, tt):
        return s.dma_start(
            out=x2b_d.ap().rearrange("(c g p) f -> g p c f", c=DC, p=128)[tt],
            in_=x2b_st.rearrange("p (c f) -> p c f", f=128))
    def s_tr_mk(cc):
        def s_tr(s, cc=cc):
            return s.dma_start_transpose(
                x2cf[:, cc * MT:(cc + 1) * MT],
                x2b_d.ap()[cc * MT:(cc + 1) * MT, :])
        return s_tr
    S.wait("sync", "s_spf", val=16 * NMT)
    S.wait("sync", "s_spb", val=16 * NMT)
    S.wait("sync", "s_ve", f"woev{NMT - 1}_7")
    for cc in range(DC):
        S.op("sync", s_tr_mk(cc), inc=(f"s_t{cc // 8}", f"tr{cc}"), n=16)
    def s_rd_mk(tt):
        def s_rd(s, tt=tt):
            return s.dma_start(out=x2rd[tt % 4][:, :],
                               in_=x2f_d.ap()[tt * 128:(tt + 1) * 128, :])
        return s_rd

    def s_st_mk(tt):
        def s_st(s, tt=tt):
            return s.dma_start(out=out_d[tt * 128:(tt + 1) * 128, :], in_=x2rd[tt % 4][:, :])
        return s_st
    for tt in range(0, NMT, 2):
        if tt < 4:
            S.wait("sync", "s_pe", f"Wp{7 * (DC // 2) + 4 * tt + 3}")
        else:
            S.wait("sync", f"s_o{tt % 4}", f"st{tt - 4}")
        S.op("sync", s_rd_mk(tt), inc=("s_xa", f"x2rd{tt}"), n=16)
        S.wait("sync", "s_ve", f"out_ready{tt}")
        S.op("sync", s_st_mk(tt), inc=(f"s_o{tt % 4}", f"st{tt}"), n=16)
    for i in range(4):
        S.wait("sync", f"s_o{i}", val=16 * (NMT // 4))

    VO(lambda v: v.memset(ones_col[:, :], 1.0))
    VO(lambda v: v.memset(ones_row[:, :], 1.0))
    VO(lambda v: v.memset(c_n30[:, :], EXPB))
    VO(lambda v: v.memset(c_eps[:, :], EPS))
    VO(lambda v: v.memset(msum3[:, :], 0.0))
    VO(lambda v: v.memset(vsum3[:, :], 0.0), label="consts")
    for cc in range(DC):
        S.wait("vector", "s_xa" if cc % 2 == 0 else "s_xb", f"wq{cc}")

        def v_fold_r(v, cc=cc):
            src = wqraw[cc % 2].rearrange("p (k g d) -> p k d g", g=4, d=128)
            return v.tensor_reduce(
                fold_t.rearrange("p (k d) -> p k d", d=128),
                src, axis=AX.X, op=ALU.add)
        VO(v_fold_r)
        VO(lambda v, cc=cc: v.tensor_copy(wqc[:, cc * 512:(cc + 1) * 512], fold_t[:, :]),
           label=f"fold{cc}")
    S.lab[("s_ve", "wqc_done")] = S.tot["s_ve"]
    for g_i in range(NG):
        S.wait("vector", "s_se", f"m2r{g_i}")
        VO(lambda v, g_i=g_i: v.scalar_tensor_tensor(
            varrow[:, :], ps_st[:, GT:2 * GT], INV_D, m2row[:, :],
            ALU.mult, ALU.subtract), label=f"varr{g_i}")
        for t2 in range(2):
            tt = g_i * 2 + t2
            S.wait("vector", "s_se", f"rstdc{tt}")
            S.wait("vector", "s_pe", f"q{g_i}_{t2}")
            if g_i >= 1:
                S.wait("vector", "s_pe", f"qtr{g_i - 1}_{t2}")
            VO(lambda v, tt=tt, t2=t2: v.tensor_scalar(
                qtm[t2][:, :], ps_q[t2][:, :], rstd[:, tt:tt + 1], None, ALU.mult),
               label=f"qev{g_i}_{t2}")
            S.wait("vector", "s_pe", f"v{g_i}_{t2}")
            VO(lambda v, tt=tt, t2=t2: v.tensor_scalar(
                vtm[:, tt * 512:(tt + 1) * 512], ps_v[t2][:, :],
                rstd[:, tt:tt + 1], None, ALU.mult), label=f"vev{g_i}_{t2}")
    for idx in range(H * 4):
        r = idx % 2
        S.wait("vector", "s_pe", f"vs{idx}_15")
        if idx >= 2:
            S.wait("vector", "s_pe", f"vt{idx - 2}")
        VO(lambda v, r=r: v.tensor_copy(vfm_st[r][:, :], ps_vv[r][:, :]),
           label=f"vfm{idx}")
        VO(lambda v, idx=idx, r=r: v.reciprocal(
            rs_cols[:, idx * 4:idx * 4 + 4], ps_sum[r][:, :]), label=f"rs{idx}")
        S.wait("vector", "s_pe", f"vt{idx}")
        qg, h = idx // 4, idx % 4
        for sub in range(4):
            qt = qg * 4 + sub
            VO(lambda v, h=h, qt=qt, sub=sub, idx=idx: v.tensor_scalar(
                vout[:, qt * 512 + h * 128: qt * 512 + (h + 1) * 128],
                ps_vt[:, sub * 128:(sub + 1) * 128],
                rs_cols[:, idx * 4 + sub: idx * 4 + sub + 1], None, ALU.mult),
               label=(f"vout{idx}" if sub == 3 else None))
        if idx == 11:
            S.wait("vector", "s_msk", val=32)
            for j in range(4):
                VO(lambda v, j=j: v.tensor_scalar(
                    jbf[:, :], vout[:, (j + 8) * 512:(j + 9) * 512],
                    msA[:, :], None, ALU.mult))
                VO(lambda v, j=j: v.scalar_tensor_tensor(
                    contrib_st[:, j * 512:(j + 1) * 512],
                    vout[:, j * 512:(j + 1) * 512], msB[:, :],
                    jbf[:, :], ALU.mult, ALU.add),
                   label=("contribA_ready" if j == 3 else None))
    for j in range(4, NMT):
        VO(lambda v, j=j: v.tensor_scalar(
            jbf[:, :], vout[:, (j + 8) * 512:(j + 9) * 512], msA[:, :], None, ALU.mult))
        VO(lambda v, j=j: v.scalar_tensor_tensor(
            contrib_st[:, j * 512:(j + 1) * 512],
            vout[:, j * 512:(j + 1) * 512], msB[:, :],
            jbf[:, :], ALU.mult, ALU.add),
           label=("contrib_ready" if j == NMT - 1 else None))
    S.wait("vector", "s_ga", val=32)
    for j in range(NMT):
        if j == 4:
            S.wait("vector", "s_gb", val=32)
        VO(lambda v, j=j: v.tensor_scalar(
            jbf[:, :], vout[:, j * 512:(j + 1) * 512], msA[:, :], None, ALU.mult))
        VO(lambda v, j=j: v.scalar_tensor_tensor(
            v_all[:, j * 1024:j * 1024 + 512],
            gbufA[:, j * 512:(j + 1) * 512], msB[:, :],
            jbf[:, :], ALU.mult, ALU.add))
        VO(lambda v, j=j: v.tensor_scalar(
            jbf[:, :], vout[:, (j + 8) * 512:(j + 9) * 512], msB[:, :], None, ALU.mult))
        VO(lambda v, j=j: v.scalar_tensor_tensor(
            v_all[:, j * 1024 + 512:(j + 1) * 1024],
            gbufB[:, j * 512:(j + 1) * 512], msA[:, :],
            jbf[:, :], ALU.mult, ALU.add), label=f"va{j}")
    S.lab[("s_ve", "vall_all")] = S.tot["s_ve"]
    for tt in range(NMT):
        for og in range(8):
            S.wait("vector", "s_pe", f"wo{tt}_{og}")
            if og == 0 and tt >= 2:
                S.wait("vector", "s_se", f"ln4_{tt - 2}")
            VO(lambda v, tt=tt, og=og: v.tensor_scalar(
                attn_t[tt % 2][:, og * 512:(og + 1) * 512], ps_wo[og % 4][:, :],
                0.0, 0.0, ALU.add, ALU.add, accum_out=mpart4[:, og:og + 1]),
               label=f"woev{tt}_{og}")
        VO(lambda v: v.reduce_sum(tmp3[:, 6:7], mpart4[:, :], axis=AX.X),
           label=f"m4s_{tt}")
        S.wait("vector", "s_se", f"ln4_{tt}")
        S.wait("vector", "s_xm", f"xm{tt}")
        if tt >= 1:
            S.wait("vector", "s_spf", f"spf{tt - 1}")
        VO(lambda v, tt=tt: v.tensor_scalar(
            lnt[:, :], attn_t[tt % 2][:, :], nm4[:, :], rstd4[:, :], ALU.add, ALU.mult))
        VO(lambda v: v.scalar_tensor_tensor(
            x2f_st[:, :], lnt[:, :], 0.0, xmine_sb[:, :], ALU.add, ALU.add,
            accum_out=tmp3[:, 7:8]), label=f"xm_used{tt}")
        S.wait("vector", "s_se", f"r2_{tt}")
        if tt >= 1:
            S.wait("vector", "s_spb", f"spb{tt - 1}")
        VO(lambda v, tt=tt: v.tensor_scalar(
            x2b_st[:, :], x2f_st[:, :], nmean2[:, tt:tt + 1], rstd2[:, tt:tt + 1],
            ALU.add, ALU.mult), label=f"x2b_ready{tt}")
    for tt in range(NMT):
        S.wait("vector", "s_pe", f"qk{tt}")
        VO(lambda v, tt=tt: v.tensor_copy(qk_all[:, tt * 512:(tt + 1) * 512],
                                          ps_f[tt % 2][:, :]), label=f"qkev{tt}")
        S.wait("vector", "s_pe", f"hh{tt}")
        VO(lambda v, tt=tt: v.tensor_copy(h_tmp[tt % 2][:, :], ps_h8[2 + tt % 2][:, :]),
           label=f"hev{tt}")
        VO(lambda v, tt=tt: v.tensor_mul(prod[:, :], qk_all[:, tt * 512:tt * 512 + 256],
                                         qk_all[:, tt * 512 + 256:(tt + 1) * 512]))
        VO(lambda v: v.tensor_reduce(se8.rearrange("p (e u) -> p e u", u=1),
                                     prod.rearrange("p (e d) -> p e d", d=32),
                                     axis=AX.X, op=ALU.add))
        VO(lambda v: v.reduce_max(tmp1[:, :], se8[:, :], axis=AX.X))
        VO(lambda v: v.tensor_scalar(tmp2[:, :], tmp1[:, :], -ISQ_DK, None, ALU.mult),
           label=f"sm{tt}")
        S.wait("vector", "s_se", f"we{tt}")
        VO(lambda v: v.reduce_sum(tmp1[:, :], we8[:, :], axis=AX.X))
        VO(lambda v: v.reciprocal(tmp2[:, :], tmp1[:, :]))
        VO(lambda v: v.tensor_scalar(we8[:, :], we8[:, :], tmp2[:, :], None, ALU.mult))
        for rr in range(16):
            VO(lambda v, rr=rr: v.tensor_copy(
                wrep.rearrange("p (e r) -> p r e", r=16)[:, rr, :], we8[:, :]))
        if tt >= 1:
            S.wait("vector", "s_pe", f"gt{tt - 1}")
        VO(lambda v, tt=tt: v.tensor_mul(gbf[:, :], h_tmp[tt % 2][:, :], wrep[:, :]),
           label=f"g{tt}")
    S.lab[("s_ve", "p5a_done")] = S.tot["s_ve"]
    for og in range(8):
        S.wait("vector", "s_se", f"fev{og}_{NMT - 1}")
        VO(lambda v: v.tensor_add(msum3[:, :], msum3[:, :], mpart[:, :]))
        VO(lambda v: v.tensor_add(vsum3[:, :], vsum3[:, :], vpart[:, :]),
           label=f"acc{og}")
    for tt in range(NMT):
        VO(lambda v, tt=tt: v.tensor_scalar(nm3[:, tt:tt + 1], msum3[:, tt:tt + 1],
                                            -INV_D, None, ALU.mult))
        VO(lambda v, tt=tt: v.tensor_scalar(tmp1[:, :], msum3[:, tt:tt + 1],
                                            INV_D, None, ALU.mult))
        VO(lambda v: v.tensor_mul(tmp2[:, :], tmp1[:, :], tmp1[:, :]))
        VO(lambda v, tt=tt: v.tensor_scalar(tmp1[:, :], vsum3[:, tt:tt + 1],
                                            INV_D, None, ALU.mult))
        VO(lambda v, tt=tt: v.tensor_sub(var3[:, tt:tt + 1], tmp1[:, :], tmp2[:, :]),
           label=f"f3_{tt}")
    for tt in range(NMT):
        S.wait("vector", "s_se", f"lnt2_{tt}")
        S.wait("vector", "s_xa" if tt % 2 == 0 else "s_xb", f"x2rd{tt}")
        VO(lambda v, tt=tt: v.tensor_add(x2rd[tt % 4][:, :], x2rd[tt % 4][:, :],
                                         (lnt2 if tt % 2 == 0 else lnt2b)[:, :]),
           label=f"out_ready{tt}")

    for g_i in range(NG):
        S.wait("scalar", "s_xf" if g_i % 2 == 0 else "s_xg", f"xf{g_i}")
        if g_i >= 1:
            S.wait("scalar", "s_pe", f"stat{g_i - 1}")
        SO(lambda s, g_i=g_i: s.activation(
            xsq[:, :], xcg[g_i % 2][:, 0:16 * GT], AF.Square), label=f"xsqr{g_i}_0")
        S.wait("scalar", "s_pe", f"sq0_{g_i}")
        SO(lambda s, g_i=g_i: s.activation(
            xsq[:, :], xcg[g_i % 2][:, 16 * GT:32 * GT], AF.Square),
           label=f"xsqr{g_i}_1")
        S.wait("scalar", "s_pe", f"stat{g_i}")
        SO(lambda s, g_i=g_i: s.activation(
            nm_row[:, g_i * GT:(g_i + 1) * GT], ps_st[:, 0:GT], AF.Copy, scale=-INV_D),
           label=f"nmr{g_i}")
        SO(lambda s: s.activation(m2row[:, :], ps_st[:, 0:GT], AF.Square, scale=INV_D),
           label=f"m2r{g_i}")
        if g_i == 0:
            S.wait("scalar", "s_pe", "csk")
            SO(lambda s: s.copy(colsk[:, :], ps_cs[:, :]), label="cske")
        for h in range(H):
            S.wait("scalar", "s_pe", f"k{g_i}_{h}")
            SO(lambda s, g_i=g_i, h=h: s.copy(
                kfm[:, h * T + g_i * GT: h * T + (g_i + 1) * GT], ps_k[h % 2][:, :]),
               label=f"kev{g_i}_{h}")
        S.wait("scalar", "s_ve", f"varr{g_i}")
        SO(lambda s: s.activation(varrow[:, :], varrow[:, :], AF.Ln,
                                  bias=c_eps[0:1, :], scale=1.0))
        SO(lambda s: s.activation(rstd_row[:, :], varrow[:, :], AF.Exp, scale=-0.5),
           label=f"rstdr{g_i}")
        for t2 in range(2):
            tt = g_i * 2 + t2
            S.wait("scalar", "s_pe", f"rtr{tt}")
            SO(lambda s, tt=tt: s.copy(rstd[:, tt:tt + 1], ps_rowf[:, :]),
               label=f"rstdc{tt}")
        SO(lambda s, g_i=g_i: s.activation(
            rstd_s[:, g_i * 2:g_i * 2 + 2], rstd[:, g_i * 2:g_i * 2 + 2],
            AF.Copy, scale=ISQ_HD), label=f"rstds{g_i}")
        if g_i == 0:
            S.wait("scalar", "s_pe", "csv")
            SO(lambda s: s.copy(colsv[:, :], ps_cs[:, :]), label="csve")
            S.wait("scalar", "s_pe", "csq")
            SO(lambda s: s.copy(colsq[:, :], ps_cs[:, :]), label="csqe")
        if g_i == 0:
            S.wait("scalar", "s_ve", "wqc_done")
        for t2 in range(2):
            S.wait("scalar", "s_pe", f"qtr{g_i}_{t2}")
            tt = g_i * 2 + t2
            for h in range(H):
                SO(lambda s, tt=tt, h=h: s.copy(
                    qfm[:, h * T + tt * 128: h * T + (tt + 1) * 128],
                    ps_tr[:, h * 128:(h + 1) * 128]),
                   label=(f"qtrc{g_i}_{t2}" if h == H - 1 else None))
    for idx in range(H * 4):
        for kt in range(NTT):
            S.wait("scalar", "s_pe", f"s{idx}_{kt}")
            if kt >= 2:
                S.wait("scalar", "s_pe", f"vs{idx}_{kt - 2}")
            elif idx > 0:
                S.wait("scalar", "s_pe", f"vs{idx - 1}_{14 + kt}")
            SO(lambda s, kt=kt: s.activation(
                expT[kt % 2][:, :], ps_s[kt % 2][:, :], AF.Exp,
                bias=c_n30[:, :], scale=rstd_s[:, kt:kt + 1]), label=f"e{idx}_{kt}")
    for j in range(NMT):
        for ag in range(2):
            vi = j * 2 + ag
            S.wait("scalar", "s_pe", f"vat{j}_{ag}")
            for k in range(4):
                a = ag * 4 + k
                SO(lambda s, j=j, a=a, k=k, vi=vi: s.copy(
                    v_allT[:, a * MT + j * 128: a * MT + (j + 1) * 128],
                    ps_va[vi % 4][:, k * 128:(k + 1) * 128]),
                   label=(f"vac{j}_{ag}" if k == 3 else None))
    for tt in range(NMT):
        S.wait("scalar", "s_ve", f"m4s_{tt}")
        SO(lambda s, tt=tt: s.activation(junk[:, :], attn_t[tt % 2][:, :], AF.Square,
                                         accum_out=tmp3[:, 2:3]))
        SO(lambda s: s.activation(nm4[:, :], tmp3[:, 6:7], AF.Copy, scale=-INV_D))
        SO(lambda s: s.activation(b4c[:, :], nm4[:, :], AF.Square))
        SO(lambda s: s.activation(b4c[:, :], b4c[:, :], AF.Copy, bias=EPS, scale=-1.0))
        SO(lambda s: s.activation(tmp3[:, 3:4], tmp3[:, 2:3], AF.Ln,
                                  bias=b4c[:, :], scale=INV_D))
        SO(lambda s: s.activation(rstd4[:, :], tmp3[:, 3:4], AF.Exp, scale=-0.5),
           label=f"ln4_{tt}")
        if tt == 0:
            S.wait("scalar", "s_ve", "vall_all")
            S.op("scalar", lambda s: s_xm(s, 0), inc=("s_xm", "xm0"), n=16, dma=True)
        S.wait("scalar", "s_ve", f"xm_used{tt}")
        if tt + 1 < NMT:
            S.op("scalar", lambda s, tt=tt: s_xm(s, tt + 1),
                 inc=("s_xm", f"xm{tt + 1}"), n=16, dma=True)
        S.op("scalar", lambda s, tt=tt: s_spf(s, tt), inc=("s_spf", f"spf{tt}"),
             n=16, dma=True)
        SO(lambda s: s.activation(junk[:, :], x2f_st[:, :], AF.Square,
                                  accum_out=tmp3[:, 4:5]))
        SO(lambda s, tt=tt: s.activation(nmean2[:, tt:tt + 1], tmp3[:, 7:8], AF.Copy,
                                         scale=-INV_D))
        SO(lambda s, tt=tt: s.activation(var2t[:, :], nmean2[:, tt:tt + 1], AF.Square))
        SO(lambda s: s.activation(var2t[:, :], var2t[:, :], AF.Copy, bias=EPS, scale=-1.0))
        SO(lambda s: s.activation(rstd2t[:, :], tmp3[:, 4:5], AF.Ln,
                                  bias=var2t[:, :], scale=INV_D))
        SO(lambda s, tt=tt: s.activation(rstd2[:, tt:tt + 1], rstd2t[:, :], AF.Exp,
                                         scale=-0.5), label=f"r2_{tt}")
        S.wait("scalar", "s_ve", f"x2b_ready{tt}")
        S.op("scalar", lambda s, tt=tt: s_spb(s, tt), inc=("s_spb", f"spb{tt}"),
             n=16, dma=True)
    for tt in range(NMT):
        S.wait("scalar", "s_ve", f"sm{tt}")
        SO(lambda s: s.activation(we8[:, :], se8[:, :], AF.Exp,
                                  bias=tmp2[:, :], scale=ISQ_DK), label=f"we{tt}")
        S.wait("scalar", "s_pe", f"gt{tt}")
        SO(lambda s, tt=tt: s.copy(gT_all[:, tt * 128:(tt + 1) * 128], ps_g[:, :]),
           label=f"gtc{tt}")
    for og in range(8):
        if og >= 1:
            S.wait("scalar", "s_ve", f"acc{og - 1}")
        for tt in range(NMT):
            S.wait("scalar", "s_pe", f"f{og}_{tt}")
            SO(lambda s, og=og, tt=tt: s.activation(
                ffn_sc[tt][:, og * 512:(og + 1) * 512], ps_f[tt][:, :], AF.Copy,
                accum_out=mpart[:, tt:tt + 1]))
            SO(lambda s, tt=tt: s.activation(
                junk5[:, :], ps_f[tt][:, :], AF.Square,
                accum_out=vpart[:, tt:tt + 1]), label=f"fev{og}_{tt}")
    S.wait("scalar", "s_spf", val=16 * NMT)
    for tt in (1, 3):
        S.op("scalar", s_rd_mk(tt), inc=("s_xb", f"x2rd{tt}"), n=16, dma=True)
    for tt in range(NMT):
        S.wait("scalar", "s_ve", f"f3_{tt}")
        SO(lambda s, tt=tt: s.activation(tmp3[:, 5:6], var3[:, tt:tt + 1], AF.Ln,
                                         bias=c_eps[:, :], scale=1.0))
        SO(lambda s, tt=tt: s.activation(rstd3[:, tt:tt + 1], tmp3[:, 5:6], AF.Exp,
                                         scale=-0.5), label=f"r3_{tt}")
        SO(lambda s, tt=tt: s.activation(nm3rs[:, :], nm3[:, tt:tt + 1], AF.Copy,
                                         scale=rstd3[:, tt:tt + 1]))
        if tt >= 2:
            S.wait("scalar", "s_ve", f"out_ready{tt - 2}")
            if (tt - 2) % 2 == 1:
                S.op("scalar", s_st_mk(tt - 2), inc=(f"s_o{(tt - 2) % 4}",
                     f"st{tt - 2}"), n=16, dma=True)
                if tt + 2 < NMT:
                    S.op("scalar", s_rd_mk(tt + 2), inc=("s_xb", f"x2rd{tt + 2}"),
                         n=16, dma=True)
        SO(lambda s, tt=tt: s.activation(
            (lnt2 if tt % 2 == 0 else lnt2b)[:, :], ffn_sc[tt][:, :], AF.Identity,
            bias=nm3rs[:, :], scale=rstd3[:, tt:tt + 1]), label=f"lnt2_{tt}")
    S.wait("scalar", "s_ve", f"out_ready{NMT - 1}")
    S.op("scalar", s_st_mk(NMT - 1), inc=(f"s_o{(NMT - 1) % 4}", f"st{NMT - 1}"),
         n=16, dma=True)

    S.wait("tensor", "s_gp", "ident")
    S.wait("tensor", "s_ve", "consts")

    def emit_qtr(g_x):
        for t2 in range(2):
            S.wait("tensor", "s_ve", f"qev{g_x}_{t2}")
            if g_x == 0 and t2 == 0:
                S.wait("tensor", "s_se", "csqe")
            if not (g_x == 0 and t2 == 0):
                pg, pt = (g_x, 0) if t2 == 1 else (g_x - 1, 1)
                S.wait("tensor", "s_se", f"qtrc{pg}_{pt}")

            def t_qtr(t, t2=t2):
                ins = None
                for h in range(H):
                    ins = t.transpose(ps_tr[:, h * 128:(h + 1) * 128],
                                      qtm[t2][:, h * 128:(h + 1) * 128],
                                      ident_b[:, :])
                return ins
            S.op("tensor", t_qtr, inc=("s_pe", f"qtr{g_x}_{t2}"))

    for g_i in range(NG):
        if g_i >= 1:
            emit_qtr(g_i - 1)
        S.wait("tensor", "s_xf" if g_i % 2 == 0 else "s_xg", f"xf{g_i}")
        if g_i >= 1:
            S.wait("tensor", "s_se", f"rstdc{2 * g_i - 1}")

        def t_sum(t, g_i=g_i):
            xc = xcg[g_i % 2]
            ins = None
            for cc in range(DC):
                ins = t.matmul(ps_st[:, 0:GT], ones_col[:, :],
                               xc[:, cc * GT:(cc + 1) * GT],
                               start=(cc == 0), stop=(cc == DC - 1))
            return ins
        S.op("tensor", t_sum, inc=("s_pe", f"sums{g_i}"))
        S.wait("tensor", "s_se", f"xsqr{g_i}_0")

        def t_sq0(t):
            ins = None
            for cc in range(16):
                ins = t.matmul(ps_st[:, GT:2 * GT], ones_col[:, :],
                               xsq[:, cc * GT:(cc + 1) * GT],
                               start=(cc == 0), stop=False)
            return ins
        S.op("tensor", t_sq0, inc=("s_pe", f"sq0_{g_i}"))
        S.wait("tensor", "s_se", f"xsqr{g_i}_1")

        def t_sq1(t):
            ins = None
            for cc in range(16):
                ins = t.matmul(ps_st[:, GT:2 * GT], ones_col[:, :],
                               xsq[:, cc * GT:(cc + 1) * GT],
                               start=False, stop=(cc == 15))
            return ins
        S.op("tensor", t_sq1, inc=("s_pe", f"stat{g_i}"))

        def t_cs_mk(w):
            def t_cs(t, w=w):
                ins = None
                for cc in range(DC):
                    ins = t.matmul(ps_cs[:, :], ones_col[:, :],
                                   w[:, cc * 512:(cc + 1) * 512],
                                   start=(cc == 0), stop=(cc == DC - 1))
                return ins
            return t_cs
        if g_i == 0:
            S.wait("tensor", "s_wkv", val=16)
            S.op("tensor", t_cs_mk(wkc), inc=("s_pe", "csk"))
            S.wait("tensor", "s_se", "cske")
        S.wait("tensor", "s_se", f"nmr{g_i}")
        for h in range(H):
            if g_i >= 1 or h >= 2:
                ppg, pph = (g_i, h - 2) if h >= 2 else (g_i - 1, h + 2)
                S.wait("tensor", "s_se", f"kev{ppg}_{pph}")

            def t_k(t, g_i=g_i, h=h):
                xc = xcg[g_i % 2]
                for cc in range(DC):
                    t.matmul(ps_k[h % 2][:, :],
                             wkc[:, cc * 512 + h * 128: cc * 512 + (h + 1) * 128],
                             xc[:, cc * GT:(cc + 1) * GT],
                             start=(cc == 0), stop=False)
                return t.matmul(ps_k[h % 2][:, :],
                                colsk[:, h * 128:(h + 1) * 128],
                                nm_row[:, g_i * GT:(g_i + 1) * GT],
                                start=False, stop=True)
            S.op("tensor", t_k, inc=("s_pe", f"k{g_i}_{h}"))
        S.wait("tensor", "s_se", f"rstdr{g_i}")
        for t2 in range(2):
            tt = g_i * 2 + t2
            if tt >= 1:
                S.wait("tensor", "s_se", f"rstdc{tt - 1}")

            def t_rtr(t, t2=t2):
                return t.transpose(ps_rowf[:, :],
                                   rstd_row[:, t2 * 128:(t2 + 1) * 128],
                                   ident_f[0:1, 0:1])
            S.op("tensor", t_rtr, inc=("s_pe", f"rtr{tt}"))
        if g_i == 0:
            S.wait("tensor", "s_wv", val=16)
            S.op("tensor", t_cs_mk(wvc), inc=("s_pe", "csv"))
            S.wait("tensor", "s_ve", "wqc_done")
            S.wait("tensor", "s_se", "csve")
            S.op("tensor", t_cs_mk(wqc), inc=("s_pe", "csq"))
            S.wait("tensor", "s_se", "csqe")
        S.wait("tensor", "s_se", f"rstds{g_i}")
        for t2 in range(2):
            tt = g_i * 2 + t2
            if g_i >= 1:
                S.wait("tensor", "s_ve", f"qev{g_i - 1}_{t2}")

            def t_q(t, t2=t2, tt=tt, g_i=g_i):
                xc = xcg[g_i % 2]
                for cc in range(DC):
                    t.matmul(ps_q[t2][:, :],
                             xc[:, cc * GT + t2 * 128: cc * GT + (t2 + 1) * 128],
                             wqc[:, cc * 512:(cc + 1) * 512],
                             start=(cc == 0), stop=False)
                return t.matmul(ps_q[t2][:, :],
                                nm_row[:, tt * 128:(tt + 1) * 128], colsq[:, :],
                                start=False, stop=True)
            S.op("tensor", t_q, inc=("s_pe", f"q{g_i}_{t2}"))
            if g_i >= 1:
                S.wait("tensor", "s_ve", f"vev{g_i - 1}_{t2}")

            def t_v(t, t2=t2, tt=tt, g_i=g_i):
                xc = xcg[g_i % 2]
                for cc in range(DC):
                    t.matmul(ps_v[t2][:, :],
                             xc[:, cc * GT + t2 * 128: cc * GT + (t2 + 1) * 128],
                             wvc[:, cc * 512:(cc + 1) * 512],
                             start=(cc == 0), stop=False)
                return t.matmul(ps_v[t2][:, :],
                                nm_row[:, tt * 128:(tt + 1) * 128], colsv[:, :],
                                start=False, stop=True)
            S.op("tensor", t_v, inc=("s_pe", f"v{g_i}_{t2}"))

    emit_qtr(NG - 1)

    def emit_vtr(idx):
        S.wait("tensor", "s_ve", f"vfm{idx}")
        if idx >= 1:
            S.wait("tensor", "s_ve", f"vout{idx - 1}")

        def t_vtr(t, idx=idx):
            r = idx % 2
            ins = None
            for sub in range(4):
                ins = t.transpose(ps_vt[:, sub * 128:(sub + 1) * 128],
                                  vfm_st[r][:, sub * 128:(sub + 1) * 128],
                                  ident_b[:, :])
            return ins
        S.op("tensor", t_vtr, inc=("s_pe", f"vt{idx}"))

    S.wait("tensor", "s_se", f"qtrc{NG - 1}_1")
    S.wait("tensor", "s_se", f"kev{NG - 1}_{H - 1}")
    S.wait("tensor", "s_ve", f"vev{NG - 1}_1")
    prev_idx = None
    for qg in range(4):
        for h in range(H):
            idx = qg * 4 + h
            r = idx % 2
            def emit_vs(kt, h=h, r=r, idx=idx):
                S.wait("tensor", "s_se", f"e{idx}_{kt}")
                if kt == 0 and idx >= 2:
                    S.wait("tensor", "s_ve", f"vfm{idx - 2}")
                    S.wait("tensor", "s_ve", f"rs{idx - 2}")

                def t_vs(t, h=h, kt=kt, r=r):
                    t.matmul(ps_vv[r][:, :],
                             vtm[:, kt * 512 + h * 128: kt * 512 + (h + 1) * 128],
                             expT[kt % 2][:, :], start=(kt == 0), stop=(kt == NTT - 1))
                    ins = None
                    for sub in range(4):
                        ins = t.matmul(ps_sum[r][:, sub:sub + 1],
                                       expT[kt % 2][:, sub * 128:(sub + 1) * 128],
                                       ones_col[:, :],
                                       start=(kt == 0 and sub == 0),
                                       stop=(kt == NTT - 1 and sub == 3))
                    return ins
                S.op("tensor", t_vs, inc=("s_pe", f"vs{idx}_{kt}"))

            for kt in range(NTT):
                if kt >= 2:
                    S.wait("tensor", "s_se", f"e{idx}_{kt - 2}")
                elif idx >= 1:
                    S.wait("tensor", "s_se", f"e{idx - 1}_{14 + kt}")

                def t_s(t, h=h, qg=qg, kt=kt):
                    return t.matmul(ps_s[kt % 2][:, :],
                                    kfm[:, h * T + kt * 128: h * T + (kt + 1) * 128],
                                    qfm[:, h * T + qg * 512: h * T + (qg + 1) * 512],
                                    start=True, stop=True)
                S.op("tensor", t_s, inc=("s_pe", f"s{idx}_{kt}"))
                if kt >= 1:
                    emit_vs(kt - 1)
            emit_vs(NTT - 1)
            if prev_idx is not None:
                emit_vtr(prev_idx)
            prev_idx = idx
    emit_vtr(prev_idx)
    S.wait("tensor", "s_ve", f"vout{H * 4 - 1}")
    S.wait("tensor", "s_se", f"e{H * 4 - 1}_{NTT - 1}")
    for j in range(NMT):
        S.wait("tensor", "s_ve", f"va{j}")
        for ag in range(2):
            vi = j * 2 + ag
            if vi >= 4:
                p = vi - 4
                S.wait("tensor", "s_se", f"vac{p // 2}_{p % 2}")

            def t_vat(t, j=j, ag=ag, vi=vi):
                ins = None
                for k in range(4):
                    a = ag * 4 + k
                    ins = t.transpose(ps_va[vi % 4][:, k * 128:(k + 1) * 128],
                                      v_all[:, j * 1024 + a * 128: j * 1024 + (a + 1) * 128],
                                      ident_b[:, :])
                return ins
            S.op("tensor", t_vat, inc=("s_pe", f"vat{j}_{ag}"))
    S.wait("tensor", "s_wo", val=16)
    S.wait("tensor", "s_se", f"vac{NMT - 1}_1")
    S.wait("tensor", "s_se", f"e{H * 4 - 1}_{NTT - 1}")
    for tt in range(NMT):
        for og in range(8):
            if tt * 8 + og >= 4:
                p = tt * 8 + og - 4
                S.wait("tensor", "s_ve", f"woev{p // 8}_{p % 8}")

            def t_wo(t, tt=tt, og=og):
                ins = None
                for a in range(8):
                    ins = t.matmul(ps_wo[og % 4][:, :],
                                   v_allT[:, a * MT + tt * 128: a * MT + (tt + 1) * 128],
                                   wob[:, a * D + og * 512: a * D + (og + 1) * 512],
                                   start=(a == 0), stop=(a == 7))
                return ins
            S.op("tensor", t_wo, inc=("s_pe", f"wo{tt}_{og}"))
    def emit_gt(tt):
        S.wait("tensor", "s_ve", f"g{tt}")
        if tt >= 1:
            S.wait("tensor", "s_se", f"gtc{tt - 1}")

        def t_gt(t):
            return t.transpose(ps_g[:, :], gbf[:, :], ident_b[:, :])
        S.op("tensor", t_gt, inc=("s_pe", f"gt{tt}"))

    S.wait("tensor", "s_qa", val=32)
    for i in range(4):
        S.wait("tensor", f"s_t{i}", val=16 * 8)
    for tt in range(NMT):
        if tt >= 2:
            S.wait("tensor", "s_ve", f"qkev{tt - 2}")

        def t_qk(t, tt=tt):
            ins = None
            for cc in range(DC):
                ins = t.matmul(ps_f[tt % 2][:, :],
                               x2cf[:, cc * MT + tt * 128: cc * MT + (tt + 1) * 128],
                               wqkb[:, cc * 512:(cc + 1) * 512],
                               start=(cc == 0), stop=(cc == DC - 1))
            return ins
        S.op("tensor", t_qk, inc=("s_pe", f"qk{tt}"))
        if tt >= 2:
            S.wait("tensor", "s_ve", f"hev{tt - 2}")

        def t_h(t, tt=tt):
            ins = None
            for cc in range(DC):
                ins = t.matmul(ps_h8[2 + tt % 2][:, :],
                               x2cf[:, cc * MT + tt * 128: cc * MT + (tt + 1) * 128],
                               Ab[:, cc * 128:(cc + 1) * 128],
                               start=(cc == 0), stop=(cc == DC - 1))
            return ins
        S.op("tensor", t_h, inc=("s_pe", f"hh{tt}"))
        if tt >= 1:
            emit_gt(tt - 1)
    emit_gt(NMT - 1)
    S.wait("tensor", "s_bmb", val=32)
    S.wait("tensor", "s_ve", "p5a_done")
    S.wait("tensor", "s_se", f"gtc{NMT - 1}")
    for og in range(8):
        for cp in range(DC // 2):
            idx = og * (DC // 2) + cp
            S.wait("tensor", f"s_w{idx % 4}", f"W{idx}")
            if cp == 0 and og >= 1:
                for tt in range(NMT):
                    S.wait("tensor", "s_se", f"fev{og - 1}_{tt}")

                    def t_w0(t, tt=tt, idx=idx):
                        ins = None
                        for cc in range(2):
                            ins = t.matmul(
                                ps_f[tt][:, :],
                                x2cf[:, cc * MT + tt * 128: cc * MT + (tt + 1) * 128],
                                Wb4[idx % 4][:, cc * 512:(cc + 1) * 512],
                                start=(cc == 0), stop=False)
                        return ins
                    if tt < NMT - 1:
                        S.op("tensor", t_w0)
                    else:
                        S.op("tensor", t_w0, inc=("s_pe", f"Wp{idx}"))
            elif cp < DC // 2 - 1:
                def t_w(t, cp=cp, idx=idx):
                    ins = None
                    for half in range(2):
                        cc = cp * 2 + half
                        for tt in range(NMT):
                            ins = t.matmul(
                                ps_f[tt][:, :],
                                x2cf[:, cc * MT + tt * 128: cc * MT + (tt + 1) * 128],
                                Wb4[idx % 4][:, half * 512:(half + 1) * 512],
                                start=(cc == 0), stop=False)
                    return ins
                S.op("tensor", t_w, inc=("s_pe", f"Wp{idx}"))
            else:
                def t_w30(t, cp=cp, idx=idx):
                    cc = cp * 2
                    ins = None
                    for tt in range(NMT):
                        ins = t.matmul(
                            ps_f[tt][:, :],
                            x2cf[:, cc * MT + tt * 128: cc * MT + (tt + 1) * 128],
                            Wb4[idx % 4][:, 0:512], start=False, stop=False)
                    return ins
                S.op("tensor", t_w30, inc=("s_pe", f"Wc30_{og}"))
                for tt in range(NMT):
                    def t_wt(t, cp=cp, idx=idx, og=og, tt=tt):
                        cc = cp * 2 + 1
                        t.matmul(ps_f[tt][:, :],
                                 x2cf[:, cc * MT + tt * 128: cc * MT + (tt + 1) * 128],
                                 Wb4[idx % 4][:, 512:1024], start=False, stop=False)
                        t.matmul(ps_f[tt][:, :], ones_row[:, :],
                                 b_bf[:, og * 512:(og + 1) * 512],
                                 start=False, stop=False)
                        return t.matmul(ps_f[tt][:, :], gT_all[:, tt * 128:(tt + 1) * 128],
                                        Bmb[:, og * 512:(og + 1) * 512],
                                        start=False, stop=True)
                    if tt < NMT - 1:
                        S.op("tensor", t_wt, inc=("s_pe", f"f{og}_{tt}"))
                    else:
                        S.op("tensor", t_wt, inc=("s_pe", f"Wp{idx}"))
                        S.lab[("s_pe", f"f{og}_{NMT - 1}")] = S.tot["s_pe"]

    sem_names = ["s_xa", "s_xb", "s_xm", "s_ct", "s_ga", "s_msk", "s_wkv", "s_wv", "s_gb",
                 "s_bmb", "s_wo", "s_qa", "s_xf",
                 "s_spf", "s_spb", "s_oa", "s_ob", "s_pe", "s_ve", "s_se",
                 "s_gp", "s_cc", "s_tr", "s_xg"] + [f"s_w{i}" for i in range(8)]
    sem_names += [f"s_t{i}" for i in range(4)]
    sem_names = [s for s in sem_names if s not in ("s_oa", "s_ob")] + [f"s_o{i}" for i in range(4)]
    import contextlib
    with contextlib.ExitStack() as stk:
        sems = {n: stk.enter_context(nc.semaphore(n)) for n in sem_names}
        block = stk.enter_context(nc.Block())
        block.sync(lambda e: S.emit("sync", e, sems))
        block.gpsimd(lambda e: S.emit("gpsimd", e, sems))
        block.tensor(lambda e: S.emit("tensor", e, sems))
        block.vector(lambda e: S.emit("vector", e, sems))
        block.scalar(lambda e: S.emit("scalar", e, sems))
    return nc


_NC_CACHE = None


def kernel(L0_attn_x, L0_attn_wq, L0_attn_wk, L0_attn_wv, L0_attn_wo,
           L0_ffn_W, L0_ffn_b, L0_ffn_Wq, L0_ffn_Wk, L0_ffn_A, L0_ffn_Bm):
    global _NC_CACHE
    x = np.asarray(L0_attn_x, dtype=np.float32)
    wq = np.asarray(L0_attn_wq, np.float32)
    wk = np.asarray(L0_attn_wk, np.float32)
    wv = np.asarray(L0_attn_wv, np.float32)
    wo = np.asarray(L0_attn_wo, np.float32)
    W = np.asarray(L0_ffn_W, np.float32)
    b = np.asarray(L0_ffn_b, np.float32)
    Wq = np.asarray(L0_ffn_Wq, np.float32)
    Wk = np.asarray(L0_ffn_Wk, np.float32)
    A = np.asarray(L0_ffn_A, np.float32)
    Bm = np.asarray(L0_ffn_Bm, np.float32)

    W_t = np.ascontiguousarray(W.T)
    wqk_t = np.ascontiguousarray(np.concatenate([Wq, Wk], axis=0).T)
    A_t = np.ascontiguousarray(A.reshape(128, D).T)
    Bm_f = np.ascontiguousarray(Bm.transpose(0, 2, 1).reshape(128, D))
    b_rs = np.ascontiguousarray(b.reshape(1, D))
    wo_c = np.ascontiguousarray(wo)

    in_maps = []
    for c in range(8):
        bi, hi = c // 2, c % 2
        in_maps.append({
            "x_tm": np.ascontiguousarray(x[bi]),
            "x_fm": np.ascontiguousarray(x[bi].T),
            "x_mine": np.ascontiguousarray(x[bi, hi * MT:(hi + 1) * MT]),
            "wq_s": np.ascontiguousarray(wq[:, hi * 2048:(hi + 1) * 2048]),
            "wk_s": np.ascontiguousarray(wk[:, hi * KV:(hi + 1) * KV]),
            "wv_s": np.ascontiguousarray(wv[:, hi * KV:(hi + 1) * KV]),
            "wo_t": wo_c,
            "W_t": W_t,
            "b_r": b_rs,
            "wqk_t": wqk_t,
            "A_t": A_t,
            "Bm_f": Bm_f,
            "msA_d": np.full((128, 1), 1.0 - hi, np.float32),
            "msB_d": np.full((128, 1), float(hi), np.float32),
        })
    if _NC_CACHE is None:
        _NC_CACHE = build_nc()
    res = run_bass_kernel_spmd(_NC_CACHE, in_maps, core_ids=list(range(8)))
    out = np.empty((4, T, D), np.float32)
    for c in range(8):
        bi, hi = c // 2, c % 2
        out[bi, hi * MT:(hi + 1) * MT] = res.results[c]["out"]
    return out
